# version 12
# baseline (speedup 1.0000x reference)
import math
import numpy as np
import ml_dtypes
from contextlib import ExitStack

import concourse.bass as bass
import concourse.mybir as mybir
from concourse.bass_utils import run_bass_kernel_spmd
from concourse.alu_op_type import AluOpType as OP

F32 = mybir.dt.float32
BF16 = mybir.dt.bfloat16
AF = mybir.ActivationFunctionType
AX = mybir.AxisListType
MAGIC = 12582912.0
LN127 = math.log(127.0)

D = 1024
HID = 4096
NH = 16
HD = 64


class Op:
    __slots__ = ("eng", "fn", "waits", "inc_sem", "inc_amt", "tag")

    def __init__(self, eng, fn, waits, inc_sem, inc_amt, tag):
        self.eng, self.fn, self.waits = eng, fn, waits
        self.inc_sem, self.inc_amt, self.tag = inc_sem, inc_amt, tag


class Prog:
    ENGS = ("sync", "gpsimd", "tensor", "vector", "scalar")

    def __init__(self, nc):
        self.nc = nc
        self.ops = []
        self.sems = {}
        self.cnt = {}

    def sem(self, name, ctx):
        self.sems[name] = ctx.enter_context(self.nc.semaphore(name))
        self.cnt[name] = 0
        return name

    AUTODELAY = ("vector", "scalar", "gpsimd")

    def op(self, eng, fn, waits=(), inc=None, amt=1, raw_inc=False, tag=""):
        wl = []
        for (s, vv) in waits:
            assert vv <= self.cnt[s], f"future token {s}={vv}>{self.cnt[s]} tag={tag}"
            if vv > 0:
                wl.append((s, vv))
        if inc is not None and not raw_inc and eng in self.AUTODELAY and eng in getattr(self, "dly", {}):
            self.ops.append(Op(eng, fn, wl, None, 1, tag))
            buf = self.dly[eng]
            half = buf.shape[1] // 2
            if eng == "scalar":
                dfn = lambda e, half=half: e.copy(buf[:, half:2 * half], buf[:, 0:half])
            else:
                dfn = lambda e, half=half: e.tensor_copy(buf[:, half:2 * half], buf[:, 0:half])
            self.cnt[inc] += amt
            tok = self.cnt[inc]
            self.ops.append(Op(eng, dfn, [], inc, amt, tag + "+d"))
            return tok
        tok = None
        if inc is not None:
            self.cnt[inc] += amt
            tok = self.cnt[inc]
        self.ops.append(Op(eng, fn, wl, inc, amt, tag))
        return tok

    def dma(self, queue, out, in_, sem, waits=(), transpose=False, slow_ok=False, tag=""):
        def fn(eng, out=out, in_=in_, transpose=transpose, slow_ok=slow_ok):
            if transpose:
                return eng.dma_start(out=out, in_=in_, transpose=True)
            if slow_ok:
                return eng.dma_start(out=out, in_=in_, allow_slow_non_contiguous=True)
            return eng.dma_start(out=out, in_=in_)
        return self.op(queue, fn, waits=waits, inc=sem, amt=16, raw_inc=True, tag=tag)

    def tok(self, eng, sem, waits=(), tag="tok"):
        return self.op(eng, lambda e: e.nop(), waits=waits, inc=sem, raw_inc=True, tag=tag)

    def delay(self, eng, inc=None, waits=(), tag="dly"):
        buf = self.dly[eng]
        half = buf.shape[1] // 2
        if eng == "scalar":
            fn = lambda e: e.copy(buf[:, half:2 * half], buf[:, 0:half])
        else:
            fn = lambda e: e.tensor_copy(buf[:, half:2 * half], buf[:, 0:half])
        return self.op(eng, fn, inc=inc, waits=waits, raw_inc=True, tag=tag)

    def barrier(self):
        toks = {}
        for e in self.ENGS:
            if e == "sync":
                toks[e] = self.op(e, lambda en: en.nop(), inc="b_" + e, raw_inc=True, tag="bar")
            else:
                toks[e] = self.op(e, lambda en: en.drain(), inc="b_" + e, raw_inc=True, tag="bar")
        for e in self.ENGS:
            waits = [("b_" + o, toks[o]) for o in self.ENGS if o != e]
            self.op(e, lambda en: en.nop(), waits=waits, tag="barw")

    def replay(self, block):
        sems = self.sems

        def make(name):
            def run(eng):
                for o in self.ops:
                    if o.eng != name:
                        continue
                    for (s, vv) in o.waits:
                        eng.wait_ge(sems[s], vv)
                    inst = o.fn(eng)
                    if o.inc_sem is not None:
                        inst.then_inc(sems[o.inc_sem], o.inc_amt)
            return run

        block.sync(make("sync"))
        block.gpsimd(make("gpsimd"))
        block.tensor(make("tensor"))
        block.vector(make("vector"))
        block.scalar(make("scalar"))


def host_constants(S):
    c = {}
    c["ident"] = np.eye(128, dtype=np.float32)
    inv = 1.0 / (10000.0 ** (np.arange(0, HD, 2) / HD))
    ang = np.arange(S)[None, :] * inv[:, None]
    cos = np.cos(ang)
    sin = np.sin(ang)
    c["cosT"] = np.concatenate([cos, cos, cos, cos], 0).astype(ml_dtypes.bfloat16)
    c["sinS"] = np.concatenate([-sin, sin, -sin, sin], 0).astype(ml_dtypes.bfloat16)
    pr = np.zeros((128, 128), dtype=np.float32)
    for o in range(128):
        pr[(o % 64 + 32) % 64 + 64 * (o // 64), o] = 1.0
    c["prot"] = pr.astype(ml_dtypes.bfloat16)
    k = np.arange(128)[:, None]
    j = np.arange(512)[None, :]
    c["cmask"] = np.concatenate([(i * 128 + k <= j) for i in range(4)], 1).astype(ml_dtypes.bfloat16)
    return c


def build_nc(S=2048, NS=2):
    NT = S // 128
    TB = S // 512
    nc = bass.Bass("TRN2", target_bir_lowering=False, detect_race_conditions=False)

    x_in = nc.dram_tensor("x", [NS * S, D], F32, kind="ExternalInput").ap()
    wqkv_in = nc.dram_tensor("w_qkv", [D, 3 * D], F32, kind="ExternalInput").ap()
    wproj_in = nc.dram_tensor("w_proj", [D, D], F32, kind="ExternalInput").ap()
    bproj_in = nc.dram_tensor("b_proj", [D], F32, kind="ExternalInput").ap()
    w1_in = nc.dram_tensor("w1", [HID, D], F32, kind="ExternalInput").ap()
    cw_in = nc.dram_tensor("conv_w", [HID, 1, 3], F32, kind="ExternalInput").ap()
    cb_in = nc.dram_tensor("conv_b", [HID], F32, kind="ExternalInput").ap()
    w2_in = nc.dram_tensor("w2", [D, HID], F32, kind="ExternalInput").ap()
    ident_in = nc.dram_tensor("ident", [128, 128], F32, kind="ExternalInput").ap()
    cosT_in = nc.dram_tensor("cosT", [128, S], BF16, kind="ExternalInput").ap()
    sinS_in = nc.dram_tensor("sinS", [128, S], BF16, kind="ExternalInput").ap()
    prot_in = nc.dram_tensor("prot", [128, 128], BF16, kind="ExternalInput").ap()
    cmask_in = nc.dram_tensor("cmask", [128, 2048], BF16, kind="ExternalInput").ap()
    out_hbm = nc.dram_tensor("out", [NS * S, D], F32, kind="ExternalOutput").ap()

    scr_xnbf = nc.dram_tensor("scr_xnbf", [S, D], BF16, kind="Internal").ap()
    scr_i1 = nc.dram_tensor("scr_i1", [S, D], BF16, kind="Internal").ap()
    scr_w1i = nc.dram_tensor("scr_w1i", [HID, D], BF16, kind="Internal").ap()
    scr_w2i = nc.dram_tensor("scr_w2i", [D, HID], BF16, kind="Internal").ap()
    scr_w1t = nc.dram_tensor("scr_w1t", [HID // 128, D // 128, 128, 128], BF16, kind="Internal").ap()
    scr_w2t = nc.dram_tensor("scr_w2t", [HID // 128, 128, D], BF16, kind="Internal").ap()
    scr_attn = nc.dram_tensor("scr_attn", [S, D], F32, kind="Internal").ap()

    ptr = [16512]

    def At(name, shape, dtype, off=None):
        sz = int(np.prod(shape[1:])) * (4 if dtype == F32 else 2)
        if off is None:
            off = ptr[0]
            ptr[0] = off + ((sz + 31) // 32) * 32
        return nc.alloc_sbuf_tensor_at(name, shape, dtype, offset=off)

    v = {}
    v["idf"] = At("idf", [128, 128], F32)
    v["prot"] = At("prot_s", [128, 128], BF16)
    v["onesf"] = At("onesf", [128, 128], F32)
    v["cmask"] = At("cmask_s", [128, 2048], BF16)
    v["bbc"] = At("bbc", [128, D], F32)
    v["cw"] = At("cw", [128, 96], F32)
    v["cb"] = At("cb", [128, 32], F32)
    v["scal"] = At("scal", [128, 8], F32)
    v["B1cols"] = At("B1cols", [128, NT], F32)
    v["rows"] = At("rows", [128, 128], F32)
    v["tiny"] = At("tiny", [128, 8], F32)
    v["M2cols"] = At("M2cols", [128, 8], F32)
    v["lncols"] = At("lncols", [128, 224], F32)
    v["dly_v"] = At("dly_v", [128, 512], F32)
    v["dly_s"] = At("dly_s", [128, 512], F32)
    v["dly_g"] = At("dly_g", [128, 448], F32)
    v["asums"] = At("asums", [128, 32], F32)

    v["I1T"] = At("I1T", [128, 8 * S], BF16)
    reg_x = ptr[0]
    v["xnT"] = At("xnT", [128, 8 * S], BF16, off=reg_x)
    v["ARaw"] = At("ARaw", [128, NH * 512], F32, off=reg_x)
    v["I2"] = At("I2", [128, 32 * 512], BF16, off=reg_x)
    ptr[0] = reg_x + max(16 * S, 32768, 32768)
    reg_qg = ptr[0]
    v["QKT"] = At("QKT", [128, 16 * S], BF16, off=reg_qg)
    v["G"] = At("G", [128, 32 * 512], F32, off=reg_qg)
    ptr[0] = reg_qg + max(32 * S, 65536)
    reg_m = ptr[0]

    def Am(name, shape, dtype, off):
        return At(name, shape, dtype, off=reg_m + off)

    v["bpr"] = Am("bpr", [1, D], F32, 0)
    v["xld"] = [Am(f"xld{i}", [128, D], F32, i * 4096) for i in range(3)]
    v["xn_buf"] = Am("xn_buf", [128, D], F32, 12288)
    v["tmpA1"] = Am("tmpA1", [128, D], F32, 16384)
    v["i1t_b"] = [Am(f"i1t{i}", [128, D], BF16, 20480 + i * 2048) for i in range(2)]
    v["xnb_b"] = [Am(f"xnb{i}", [128, D], BF16, 24576 + i * 2048) for i in range(2)]
    v["wtile"] = [At(f"wtile{i}", [128, HID], F32, off=reg_qg + i * 16384) for i in range(2)]
    v["wq_b"] = [At(f"wq{i}", [128, HID], BF16, off=reg_qg + 32768 + i * 8192) for i in range(2)]
    v["wtt_b"] = [At(f"wtt{i}", [128, HID], BF16, off=reg_qg + 49152 + i * 8192) for i in range(2)]
    v["wqkv_b"] = [Am(f"wqkvb{i}", [128, 128], BF16, 28672 + i * 256) for i in range(2)]
    v["rtmpA"] = Am("rtmpA", [128, 512], BF16, 29184)
    v["rtmpB"] = Am("rtmpB", [128, 512], BF16, 30208)
    v["cosT"] = Am("cosT_s", [128, S], BF16, 31232)
    v["sinS"] = Am("sinS_s", [128, S], BF16, 31232 + 2 * S)
    v["vaug"] = Am("vaug", [128, NH * 65 * (S // 128)], BF16, 0)
    v["wv_sb"] = Am("wv_sb", [128, 8 * D], BF16, 33280)
    v["anT"] = Am("anT", [128, NH * 512], BF16, 33280)
    v["pT"] = [Am(f"pT{i}", [128, 512], BF16, 49664 + i * 1024) for i in range(2)]
    v["zr"] = Am("zr", [128, 512], F32, 51712)
    v["zscr"] = Am("zscr", [128, 512], F32, 53760)
    v["wp_b"] = [Am(f"wp{i}", [128, D], BF16, 55808 + i * 2048) for i in range(2)]
    v["att_ev"] = Am("attev", [128, D], F32, 59904)
    v["B1"] = Am("B1", [128, S], F32, 0)
    v["w1t_b"] = [Am(f"w1t{i}", [128, 1024], BF16, 8192 + i * 2048) for i in range(2)]
    v["w2t_b"] = [Am(f"w2t{i}", [128, D], BF16, 12288 + i * 2048) for i in range(2)]
    v["h_sb"] = [Am(f"hsb{i}", [128, 516], F32, 16384 + i * 2080) for i in range(2)]
    v["cv1"] = Am("cv1", [128, 512], F32, 20608)
    v["macc"] = Am("macc", [128, 512], F32, 22656)
    v["macc2"] = Am("macc2", [128, 512], F32, 51328)
    v["s2bc"] = Am("s2bc", [128, 512], F32, 24704)
    v["yev"] = [Am(f"yev{i}", [128, D], F32, 26752 + i * 4096) for i in range(2)]
    v["xres"] = [Am(f"xres{i}", [128, D], F32, 34944 + i * 4096) for i in range(2)]
    v["ares"] = [Am(f"ares{i}", [128, D], F32, 43136 + i * 4096) for i in range(2)]
    assert S >= 1024 or True
    assert reg_m + 64000 <= 229376, f"SBUF overflow {reg_m + 64000}"

    v["pb"] = [nc.alloc_psum_tensor(f"pb{i}", [128, 512], F32) for i in range(8)]
    for k in ("x_in wqkv_in wproj_in bproj_in w1_in cw_in cb_in w2_in ident_in cosT_in "
              "sinS_in prot_in cmask_in out_hbm scr_xnbf scr_i1 scr_w1i scr_w2i scr_w1t "
              "scr_w2t scr_attn").split():
        v[k] = locals()[k]
    v["S"], v["NS"], v["NT"], v["TB"] = S, NS, NT, TB

    with ExitStack() as ctx:
        p = Prog(nc)
        p.dly = {"vector": v["dly_v"], "scalar": v["dly_s"], "gpsimd": v["dly_g"]}
        for s in ("sd", "sa", "sg", "st", "sv", "ss", "sc"):
            p.sem(s, ctx)
        for e in Prog.ENGS:
            p.sem("b_" + e, ctx)
        block = ctx.enter_context(nc.Block())
        emit(p, v)
        p.replay(block)
    return nc


def emit(p, v):
    S, NS, NT, TB = v["S"], v["NS"], v["NT"], v["TB"]
    pb = v["pb"]
    idf, prot, onesf = v["idf"], v["prot"], v["onesf"]
    cosT, sinS, cmask = v["cosT"], v["sinS"], v["cmask"]
    bbc, cw, cb, scal = v["bbc"], v["cw"], v["cb"], v["scal"]
    B1cols, rows, bpr = v["B1cols"], v["rows"], v["bpr"]
    tiny, M2cols = v["tiny"], v["M2cols"]
    lncols = v["lncols"]
    I1T, xnT, ARaw, I2, QKT, G = v["I1T"], v["xnT"], v["ARaw"], v["I2"], v["QKT"], v["G"]
    xld, xn_buf, tmpA1 = v["xld"], v["xn_buf"], v["tmpA1"]
    i1t_b, xnb_b = v["i1t_b"], v["xnb_b"]
    wtile, asums, wq_b, wtt_b = v["wtile"], v["asums"], v["wq_b"], v["wtt_b"]
    wqkv_b, wv_sb, rtmpA, rtmpB = v["wqkv_b"], v["wv_sb"], v["rtmpA"], v["rtmpB"]
    vaug, anT, pT, zr, zscr = v["vaug"], v["anT"], v["pT"], v["zr"], v["zscr"]
    wp_b, att_ev = v["wp_b"], v["att_ev"]
    w1t_b, w2t_b, h_sb, cv1 = v["w1t_b"], v["w2t_b"], v["h_sb"], v["cv1"]
    macc, s2bc, B1 = v["macc"], v["s2bc"], v["B1"]
    macc2 = v["macc2"]
    yev, xres, ares = v["yev"], v["xres"], v["ares"]
    x_in, wqkv_in, wproj_in, bproj_in = v["x_in"], v["wqkv_in"], v["wproj_in"], v["bproj_in"]
    w1_in, cw_in, cb_in, w2_in = v["w1_in"], v["cw_in"], v["cb_in"], v["w2_in"]
    ident_in, cosT_in, sinS_in = v["ident_in"], v["cosT_in"], v["sinS_in"]
    prot_in, cmask_in, out_hbm = v["prot_in"], v["cmask_in"], v["out_hbm"]
    scr_xnbf, scr_i1 = v["scr_xnbf"], v["scr_i1"]
    scr_w1i, scr_w2i, scr_w1t, scr_w2t = v["scr_w1i"], v["scr_w2i"], v["scr_w1t"], v["scr_w2t"]
    scr_attn = v["scr_attn"]

    p.op("vector", lambda e: e.memset(p.dly["vector"][:, :], 0.0), raw_inc=True, tag="dlyz_v0")
    p.op("gpsimd", lambda e: e.memset(p.dly["gpsimd"][:, :], 0.0), raw_inc=True, tag="dlyz_g0")
    p.op("scalar", lambda e: e.memzero(p.dly["scalar"][:, :]), raw_inc=True, tag="dlyz_s0")
    for (dst, src) in [(idf[:, :], ident_in),
                       (prot[:, :], prot_in), (cmask[:, :], cmask_in),
                       (bpr[:, :], bproj_in.rearrange("(o d) -> o d", o=1))]:
        tok_const = p.dma("sync", dst, src, "sd", tag="const")
    p.dma("sync", cw[:, :].rearrange("p (ch o j) -> p ch o j", ch=32, o=1),
          cw_in.rearrange("(ch p) o j -> p ch o j", p=128), "sd", slow_ok=True, tag="cw")
    tok_const = p.dma("sync", cb[:, :], cb_in.rearrange("(ch p) -> p ch", p=128), "sd",
                      slow_ok=True, tag="cb")
    tok_ones = p.op("gpsimd", lambda e: e.memset(onesf[:, :], 1.0), inc="sc", tag="ones")

    p.op("gpsimd", lambda e: e.memset(scal[:, 7:8], -MAGIC), inc="sc", tag="negmag")
    p.op("gpsimd", lambda e: e.memset(scal[:, 5:6], LN127), inc="sc", tag="ln127")

    t = p.op("tensor", lambda e: e.matmul(pb[0][:, :], onesf[0:1, :], bpr[0:1, 0:512], start=True, stop=True),
             waits=[("sd", tok_const), ("sc", tok_ones)], tag="bbc0")
    t = p.op("tensor", lambda e: e.matmul(pb[1][:, :], onesf[0:1, :], bpr[0:1, 512:1024], start=True, stop=True),
             inc="st", tag="bbc1")
    p.delay("vector", waits=[("st", t)])
    p.op("vector", lambda e: e.tensor_copy(bbc[:, 0:512], pb[0][:, :]), tag="bbcE0")
    tok_bbc = p.op("vector", lambda e: e.tensor_copy(bbc[:, 512:1024], pb[1][:, :]), inc="sv", tag="bbcE1")

    def wquant(win, rowsN, colsN, scr_int, col_s, col_m):
        ntile = rowsN // 128
        red_toks = {}
        for i in range(ntile):
            wl = []
            if i >= 2:
                wl.append(("sv", red_toks[i - 2]))
            tk = p.dma("sync", wtile[i % 2][:, 0:colsN], win[i * 128:(i + 1) * 128, 0:colsN], "sd",
                       waits=wl, tag=f"wqL{i}")
            red_toks[i] = p.op("vector", lambda e, i=i: e.reduce_sum(
                asums[:, i:i + 1], wtile[i % 2][:, 0:colsN], axis=AX.X, apply_absolute_value=True),
                waits=[("sd", tk)], inc="sv", tag=f"wqS{i}")
        t1 = p.op("vector", lambda e, n=ntile: e.reduce_sum(tiny[:, 0:1], asums[:, 0:n], axis=AX.X),
                  inc="sv", tag="wqRS")
        t1 = p.delay("vector", inc="sv")
        t2 = p.op("tensor", lambda e: e.transpose(pb[2][0:1, 0:128], tiny[:, 0:1], idf[:, :]),
                  waits=[("sv", t1)], inc="st", tag="wqT")
        p.delay("vector", waits=[("st", t2)])
        p.op("vector", lambda e: e.tensor_copy(rows[0:1, 0:128], pb[2][0:1, 0:128]), tag="wqTE")
        p.delay("vector")
        p.op("vector", lambda e: e.reduce_sum(rows[0:1, 1:2], rows[0:1, 0:128], axis=AX.X), tag="wqS2")
        p.delay("vector")
        p.op("vector", lambda e, n=1.0 / (rowsN * colsN): e.tensor_scalar_mul(rows[0:1, 2:3], rows[0:1, 1:2], n),
             tag="wqMW")
        tmw = p.delay("vector", inc="sv")
        p.op("scalar", lambda e: e.activation(rows[0:1, 7:8], rows[0:1, 2:3], AF.Ln),
             waits=[("sv", tmw)], tag="wqLn")
        p.delay("scalar")
        p.op("scalar", lambda e: e.activation(rows[0:1, 3:4], rows[0:1, 7:8], AF.Exp, scale=-1.0),
             tag="wqEx")
        tre = p.delay("scalar", inc="ss")
        p.op("vector", lambda e: e.tensor_mul(rows[0:1, 4:5], rows[0:1, 2:3], rows[0:1, 3:4]),
             waits=[("ss", tre)], tag="wqN1")
        p.delay("vector")
        p.op("vector", lambda e: e.tensor_scalar(rows[0:1, 4:5], rows[0:1, 4:5], -1.0, 2.0, OP.mult, OP.add),
             tag="wqN2")
        p.delay("vector")
        p.op("vector", lambda e: e.tensor_mul(rows[0:1, 5:6], rows[0:1, 3:4], rows[0:1, 4:5]), tag="wqN3")
        p.op("vector", lambda e: e.tensor_scalar_mul(rows[0:1, 6:7], rows[0:1, 2:3], 1.0 / 127.0),
             tag="wqM127")
        t5 = p.delay("vector", inc="sv")
        t6 = p.op("tensor", lambda e: e.matmul(pb[3][:, 0:1], onesf[0:1, :], rows[0:1, 5:6], start=True, stop=True),
                  waits=[("sv", t5)], tag="wqB1")
        t6 = p.op("tensor", lambda e: e.matmul(pb[3][:, 1:2], onesf[0:1, :], rows[0:1, 6:7], start=True, stop=True),
                  inc="st", tag="wqB2")
        p.delay("vector", waits=[("st", t6)])
        p.op("vector", lambda e, c=col_s: e.tensor_copy(scal[:, c:c + 1], pb[3][:, 0:1]), tag="wqBE1")
        p.op("vector", lambda e, c=col_m: e.tensor_copy(scal[:, c:c + 1], pb[3][:, 1:2]), tag="wqBE2")
        t7 = p.delay("vector", inc="sv")
        q_toks = {}
        st_toks = {}
        tok_store = None
        for i in range(ntile):
            wl = [("sv", t7)]
            if i >= 2:
                wl.append(("sv", q_toks[i - 2]))
            tk = p.dma("sync", wtile[i % 2][:, 0:colsN], win[i * 128:(i + 1) * 128, 0:colsN], "sd",
                       waits=wl, tag=f"wq2L{i}")
            p.op("vector", lambda e, i=i, c=col_s, n=colsN: e.tensor_scalar(
                wtile[i % 2][:, 0:n], wtile[i % 2][:, 0:n], scal[:, c:c + 1], MAGIC, OP.mult, OP.add),
                waits=[("sd", tk)], tag=f"wq2A{i}")
            p.op("vector", lambda e, i=i, n=colsN: e.tensor_scalar(
                wtile[i % 2][:, 0:n], wtile[i % 2][:, 0:n], MAGIC, 1.0, OP.subtract, OP.min), tag=f"wq2B{i}")
            wl2 = []
            if i >= 2:
                wl2.append(("sa", st_toks[i - 2]))
            q_toks[i] = p.op("vector", lambda e, i=i, n=colsN: e.tensor_scalar(
                wq_b[i % 2][:, 0:n], wtile[i % 2][:, 0:n], -1.0, None, OP.max),
                waits=wl2, inc="sv", tag=f"wq2C{i}")
            tok_store = p.dma("scalar", scr_int[i * 128:(i + 1) * 128, 0:colsN], wq_b[i % 2][:, 0:colsN], "sa",
                              waits=[("sv", q_toks[i])], tag=f"wq2S{i}")
            st_toks[i] = tok_store
        return tok_store

    tok_w1i = wquant(w1_in, HID, D, scr_w1i, 0, 1)
    tok_w2i = wquant(w2_in, D, HID, scr_w2i, 2, 3)

    prev = {}
    tok_w1t = None
    for d in range(8):
        wl = [("sa", tok_w1i)]
        if d >= 2:
            wl.append(("sa", prev[d - 2]))
        tk = p.dma("sync", wtt_b[d % 2][:, 0:4096], scr_w1i[:, d * 128:(d + 1) * 128], "sd",
                   waits=wl, transpose=True, tag=f"w1tT{d}")
        tok_w1t = p.dma("scalar", scr_w1t[:, d, :, :].rearrange("hc p c -> p hc c"),
                        wtt_b[d % 2][:, 0:4096].rearrange("p (hc c) -> p hc c", hc=32), "sa",
                        waits=[("sd", tk)], tag=f"w1tS{d}")
        prev[d] = tok_w1t
    prev = {}
    tok_w2t = None
    for hc in range(32):
        wl = [("sa", tok_w2i), ("sa", tok_w1t)]
        if hc >= 2:
            wl.append(("sa", prev[hc - 2]))
        tk = p.dma("sync", wtt_b[hc % 2][:, 0:1024], scr_w2i[:, hc * 128:(hc + 1) * 128], "sd",
                   waits=wl, transpose=True, tag=f"w2tT{hc}")
        tok_w2t = p.dma("scalar", scr_w2t[hc, :, :], wtt_b[hc % 2][:, 0:1024], "sa",
                        waits=[("sd", tk)], tag=f"w2tS{hc}")
        prev[hc] = tok_w2t

    p.barrier()

    prev_out_tok = None
    for sq in range(NS):
        xs_base = sq * S
        wq_pre = {}
        for k in (0, 1):
            wq_pre[k] = p.dma("gpsimd", wqkv_b[k][:, :],
                              wqkv_in[k * 128:(k + 1) * 128, 0:128], "sg",
                              waits=([] if prev_out_tok is None else [("sa", prev_out_tok)]),
                              tag=f"qwpre{k}")
        ld_toks = {}
        sq_toks = {}
        red_toks = {}
        for i in range(min(3, NT)):
            ld_toks[i] = p.dma("sync", xld[i % 3][:, :], x_in[xs_base + i * 128: xs_base + (i + 1) * 128, :],
                               "sd", tag=f"lnL{i}")
        for i in range(NT):
            b = xld[i % 3]
            wl = [("sd", ld_toks[i])]
            wsq = list(wl)
            if i >= 1:
                wsq.append(("sv", sq_toks[i - 1]))
            tsq0 = p.op("scalar", lambda e, b=b: e.activation(xn_buf[:, :], b[:, :], AF.Square),
                        waits=wsq, inc="ss", tag="lnSQ")
            p.op("vector", lambda e, i=i: e.reduce_sum(lncols[:, 16 + i:17 + i], xn_buf[:, :], axis=AX.X),
                 waits=[("ss", tsq0)], tag="lnSQr")
            sq_toks[i] = p.op("vector", lambda e, b=b, i=i: e.reduce_sum(lncols[:, i:i + 1], b[:, :], axis=AX.X),
                              waits=wl, inc="sv", tag="lnS")
            p.op("vector", lambda e, b=b, i=i: e.reduce_max(lncols[:, 32 + i:33 + i], b[:, :], axis=AX.X),
                 tag="lnMX")
            red_toks[i] = p.op("vector", lambda e, b=b, i=i: e.tensor_reduce(lncols[:, 48 + i:49 + i], b[:, :],
                                                                             axis=AX.X, op=OP.min),
                               inc="sv", tag="lnMN")
            if i + 3 < NT:
                ld_toks[i + 3] = p.dma("sync", xld[(i + 3) % 3][:, :],
                                       x_in[xs_base + (i + 3) * 128: xs_base + (i + 4) * 128, :], "sd",
                                       waits=[("sv", red_toks[i])], tag=f"lnL{i + 3}")
        C = lncols
        n = NT
        D_ = p.delay
        p.op("vector", lambda e: e.tensor_scalar_mul(C[:, 64:64 + n], C[:, 0:n], 1.0 / D), tag="p2m")
        D_("vector")
        p.op("vector", lambda e: e.tensor_mul(C[:, 80:80 + n], C[:, 64:64 + n], C[:, 64:64 + n]), tag="p2mm")
        p.op("vector", lambda e: e.tensor_scalar_mul(C[:, 96:96 + n], C[:, 16:16 + n], 1.0 / D), tag="p2sq")
        D_("vector")
        p.op("vector", lambda e: e.tensor_sub(C[:, 96:96 + n], C[:, 96:96 + n], C[:, 80:80 + n]), tag="p2var")
        D_("vector")
        p.op("vector", lambda e: e.tensor_scalar_add(C[:, 96:96 + n], C[:, 96:96 + n], 1e-5), tag="p2vpe")
        tvpe = D_("vector", inc="sv")
        p.op("scalar", lambda e: e.activation(C[:, 192:192 + n], C[:, 96:96 + n], AF.Ln),
             waits=[("sv", tvpe)], tag="p2ln")
        D_("scalar")
        p.op("scalar", lambda e: e.activation(C[:, 112:112 + n], C[:, 192:192 + n], AF.Exp, scale=-0.5),
             tag="p2r0")
        tr0 = D_("scalar", inc="ss")
        p.op("vector", lambda e: e.tensor_mul(C[:, 80:80 + n], C[:, 112:112 + n], C[:, 112:112 + n]),
             waits=[("ss", tr0)], tag="p2n1")
        D_("vector")
        p.op("vector", lambda e: e.tensor_mul(C[:, 80:80 + n], C[:, 80:80 + n], C[:, 96:96 + n]), tag="p2n2")
        D_("vector")
        p.op("vector", lambda e: e.tensor_scalar(C[:, 80:80 + n], C[:, 80:80 + n], -0.5, 1.5, OP.mult, OP.add),
             tag="p2n3")
        D_("vector")
        p.op("vector", lambda e: e.tensor_mul(C[:, 128:128 + n], C[:, 112:112 + n], C[:, 80:80 + n]), tag="p2r")
        p.op("vector", lambda e: e.tensor_sub(C[:, 176:176 + n], C[:, 32:32 + n], C[:, 64:64 + n]), tag="p2dm")
        p.op("vector", lambda e: e.tensor_sub(C[:, 80:80 + n], C[:, 64:64 + n], C[:, 48:48 + n]), tag="p2dn")
        D_("vector")
        p.op("vector", lambda e: e.tensor_max(C[:, 176:176 + n], C[:, 176:176 + n], C[:, 80:80 + n]), tag="p2M1")
        D_("vector")
        p.op("vector", lambda e: e.tensor_mul(C[:, 176:176 + n], C[:, 176:176 + n], C[:, 128:128 + n]), tag="p2M1r")
        D_("vector")
        p.op("vector", lambda e: e.tensor_scalar_max(C[:, 144:144 + n], C[:, 176:176 + n], 1e-5), tag="p2M1c")
        tM1 = D_("vector", inc="sv")
        p.op("scalar", lambda e: e.activation(C[:, 192:192 + n], C[:, 144:144 + n], AF.Ln),
             waits=[("sv", tM1)], tag="p2l2")
        D_("scalar")
        p.op("scalar", lambda e: e.activation(C[:, 208:208 + n], C[:, 192:192 + n], AF.Exp,
                                              scale=-1.0, bias=scal[:, 5:6]),
             tag="p2s10")
        ts10 = D_("scalar", inc="ss")
        p.op("vector", lambda e: e.tensor_scalar_mul(C[:, 80:80 + n], C[:, 208:208 + n], 1.0 / 127.0),
             waits=[("ss", ts10)], tag="p2q1")
        D_("vector")
        p.op("vector", lambda e: e.tensor_mul(C[:, 80:80 + n], C[:, 144:144 + n], C[:, 80:80 + n]), tag="p2q2")
        D_("vector")
        p.op("vector", lambda e: e.tensor_scalar(C[:, 80:80 + n], C[:, 80:80 + n], -1.0, 2.0, OP.mult, OP.add),
             tag="p2q3")
        D_("vector")
        p.op("vector", lambda e: e.tensor_mul(C[:, 160:160 + n], C[:, 208:208 + n], C[:, 80:80 + n]), tag="p2s1")
        D_("vector")
        p.op("vector", lambda e: e.tensor_scalar_mul(B1cols[:, 0:n], C[:, 144:144 + n], scal[:, 1:2]),
             tag="p2B1")
        tp2 = D_("vector", inc="sv")
        ld2 = {}
        i1_toks = {}
        xnb_toks = {}
        tok_lnst = None
        for i in range(min(3, NT)):
            ld2[i] = p.dma("sync", xld[i % 3][:, :], x_in[xs_base + i * 128: xs_base + (i + 1) * 128, :],
                           "sd", waits=[("sv", red_toks[NT - 1])], tag=f"ln3L{i}")
        for i in range(NT):
            b = xld[i % 3]
            wxn = [("sd", ld2[i]), ("sv", tp2)]
            if i >= 1:
                wxn.append(("sv", i1_toks[i - 1]))
                wxn.append(("ss", xnb_toks[i - 1]))
            p.op("vector", lambda e, b=b, i=i: e.tensor_scalar(xn_buf[:, :], b[:, :], C[:, 64 + i:65 + i],
                                                               C[:, 128 + i:129 + i], OP.subtract, OP.mult),
                 waits=wxn, tag="ln3XN")
            i1_toks[i] = p.op("vector", lambda e, i=i: e.tensor_scalar(tmpA1[:, :], xn_buf[:, :],
                                                                      C[:, 160 + i:161 + i], MAGIC,
                                                                      OP.mult, OP.add), inc="sv", tag="ln3I1A")
            wst = [("sv", i1_toks[i])]
            if i >= 2:
                wst.append(("sa", p.cnt["sa"]))
            p.op("scalar", lambda e, i=i: e.activation(i1t_b[i % 2][:, :], tmpA1[:, :], AF.Identity,
                                                       bias=scal[:, 7:8]), waits=wst, tag="ln3I1B")
            xnb_toks[i] = p.op("scalar", lambda e, i=i: e.copy(xnb_b[i % 2][:, :], xn_buf[:, :]),
                               inc="ss", tag="ln3XNB")
            p.dma("scalar", scr_i1[i * 128:(i + 1) * 128, :], i1t_b[i % 2][:, :], "sa",
                  waits=[("ss", xnb_toks[i])], tag=f"ln3SI{i}")
            tok_lnst = p.dma("scalar", scr_xnbf[i * 128:(i + 1) * 128, :], xnb_b[i % 2][:, :], "sa",
                             tag=f"ln3SX{i}")
            if i + 3 < NT:
                ld2[i + 3] = p.dma("sync", xld[(i + 3) % 3][:, :],
                                   x_in[xs_base + (i + 3) * 128: xs_base + (i + 4) * 128, :], "sd",
                                   waits=[("sv", i1_toks[i])], tag=f"ln3L{i + 3}")
        tok_tr = None
        for d in range(8):
            p.dma("sync", xnT[:, d * S:(d + 1) * S], scr_xnbf[:, d * 128:(d + 1) * 128], "sd",
                  waits=[("sa", tok_lnst)], transpose=True, tag=f"trX{d}")
            tok_tr = p.dma("sync", I1T[:, d * S:(d + 1) * S], scr_i1[:, d * 128:(d + 1) * 128], "sd",
                           transpose=True, tag=f"trI{d}")

        p.barrier()

        ev_tok = {}
        k_idx = 0
        mm_g = {}
        wq_toks = {}
        for oc in range(16):
            pset = (oc % 2) * 4
            for d in range(8):
                if k_idx < 2:
                    wq_toks[k_idx] = wq_pre[k_idx]
                else:
                    wq_toks[k_idx] = p.dma("gpsimd", wqkv_b[k_idx % 2][:, :],
                                           wqkv_in[d * 128:(d + 1) * 128, oc * 128:(oc + 1) * 128],
                                           "sg", waits=[("st", mm_g[k_idx - 2])], tag=f"qw{oc}_{d}")
                wl2 = [("sg", wq_toks[k_idx]), ("sd", tok_tr)]
                if d == 0 and oc >= 2:
                    wl2.append(("ss", ev_tok[oc - 2]))
                for tb in range(TB):
                    tk = p.op("tensor", lambda e, pset=pset, tb=tb, k=k_idx, d=d:
                              e.matmul(pb[pset + tb][:, :], wqkv_b[k % 2][:, :],
                                       xnT[:, d * S + tb * 512: d * S + tb * 512 + 512],
                                       start=(d == 0), stop=(d == 7)),
                              waits=(wl2 if tb == 0 else []),
                              inc=("st" if tb == TB - 1 else None), tag=f"qmm{oc}_{d}_{tb}")
                mm_g[k_idx] = tk
                k_idx += 1
            wl_ev = [("st", mm_g[k_idx - 1])]
            if oc == 0:
                wl_ev.append(("sa", tok_w2t))
            p.delay("scalar", waits=wl_ev)
            for tb in range(TB):
                p.op("scalar", lambda e, oc=oc, tb=tb, pset=pset:
                     e.copy(QKT[:, oc * S + tb * 512: oc * S + tb * 512 + 512], pb[pset + tb][:, :]),
                     tag=f"qev{oc}_{tb}")
            ev_tok[oc] = p.delay("scalar", inc="ss")
        tok_qkev = ev_tok[15]
        p.dma("sync", cosT[:, :], cosT_in, "sd", tag="cosL")
        tok_rtab = p.dma("sync", sinS[:, :], sinS_in, "sd", tag="sinL")
        rj = 0
        cmb = {}
        for oc in range(16):
            for tb in range(TB):
                sl = slice(oc * S + tb * 512, oc * S + tb * 512 + 512)
                tsl = slice(tb * 512, tb * 512 + 512)
                wl = [("ss", tok_qkev)]
                if rj >= 2:
                    wl.append(("sv", cmb[rj - 2]))
                trt = p.op("tensor", lambda e, sl=sl, rj=rj:
                           e.matmul(pb[rj % 2][:, :], prot[:, :], QKT[:, sl], start=True, stop=True),
                           waits=wl, inc="st", tag=f"rp{rj}")
                p.op("vector", lambda e, sl=sl, tsl=tsl: e.tensor_mul(rtmpA[:, :], QKT[:, sl], cosT[:, tsl]),
                     waits=[("st", trt), ("sd", tok_rtab)], tag="rpA")
                p.op("vector", lambda e, tsl=tsl, rj=rj: e.tensor_mul(rtmpB[:, :], pb[rj % 2][:, :], sinS[:, tsl]),
                     tag="rpB")
                cmb[rj] = p.op("vector", lambda e, sl=sl: e.tensor_add(QKT[:, sl], rtmpA[:, :], rtmpB[:, :]),
                               inc="sv", tag="rpC")
                rj += 1
        tok_rope = p.delay("vector", inc="sv")
        tok_wv = None
        for d in range(8):
            tok_wv = p.dma("gpsimd", wv_sb[:, d * D:(d + 1) * D],
                           wqkv_in[d * 128:(d + 1) * 128, 2048:3072], "sg",
                           waits=[("sv", tok_rope)], tag=f"wv{d}")
        vev = {}
        vones = {}
        for i in range(NT):
            pset = 4 + (i % 2) * 2
            wl = [("sg", tok_wv), ("sd", tok_tr)]
            if i >= 2:
                wl.append(("ss", vev[i - 2]))
            for d in range(8):
                for hh in range(2):
                    tmm = p.op("tensor", lambda e, pset=pset, i=i, d=d, hh=hh:
                               e.matmul(pb[pset + hh][:, :],
                                        xnT[:, d * S + i * 128: d * S + i * 128 + 128],
                                        wv_sb[:, d * D + hh * 512: d * D + hh * 512 + 512],
                                        start=(d == 0), stop=(d == 7)),
                               waits=(wl if (d == 0 and hh == 0) else []),
                               inc=("st" if (d == 7 and hh == 1) else None), tag=f"vmm{i}_{d}_{hh}")

            def vev_fn(e, i=i, pset=pset):
                dst = vaug[:, i * 1040:(i + 1) * 1040].rearrange("p (h c) -> p h c", h=16)[:, :, 0:64]
                e.copy(dst[:, 0:8, :], pb[pset][:, :].rearrange("p (h c) -> p h c", h=8))
                return e.copy(dst[:, 8:16, :], pb[pset + 1][:, :].rearrange("p (h c) -> p h c", h=8))
            p.delay("scalar", waits=[("st", tmm)])
            p.op("scalar", vev_fn, tag=f"vev{i}")
            vev[i] = p.delay("scalar", inc="ss")
            vones[i] = p.op("gpsimd", lambda e, i=i: e.memset(
                vaug[:, i * 1040:(i + 1) * 1040].rearrange("p (h c) -> p h c", h=16)[:, :, 64:65], 1.0),
                waits=[("ss", vev[i])], inc="sc", tag=f"von{i}")
        tok_vaug = vones[NT - 1]

        p.barrier()

        exp_hist = []
        pv_hist = []
        gj = 0
        last_exp_ss = 0
        evA = {}
        prev_prj = {}
        last_ln_ss = 0
        tok_prjE = 0
        for qt in range(S // 512):
            nch = 4 * (qt + 1)
            for h in range(NH):
                base = (h % 2) * 64
                qoff = (h // 2) * S + qt * 512
                koff = (8 + h // 2) * S
                avb = 2 + (h % 2)
                for c in range(nch):
                    sb = gj % 2
                    wl = [("sv", tok_rope), ("sc", tok_vaug)]
                    if tok_prjE and h == 0 and c < 2:
                        wl.append(("sv", tok_prjE))
                    if gj >= 2:
                        wl.append(exp_hist[gj - 2])
                    tmm = p.op("tensor", lambda e, base=base, koff=koff, qoff=qoff, c=c, sb=sb:
                               e.matmul(pb[sb][:, :],
                                        QKT[:, koff + c * 128: koff + c * 128 + 128][base:base + 64, :],
                                        QKT[:, qoff: qoff + 512][base:base + 64, :],
                                        start=True, stop=True),
                               waits=wl, inc="st", tag=f"sT{qt}_{h}_{c}")
                    wle = [("st", tmm)]
                    if gj >= 2:
                        wle.append(("st", pv_hist[gj - 2]))
                    p.delay("scalar", waits=wle)
                    p.op("scalar", lambda e, sb=sb: e.activation(pT[sb][:, :], pb[sb][:, :], AF.Exp,
                                                                 scale=0.125),
                         tag=f"ex{qt}_{h}_{c}")
                    texp = p.delay("scalar", inc="ss")
                    last_exp_ss = texp
                    dep = ("ss", texp)
                    if c >= 4 * qt:
                        ci = c - 4 * qt
                        p.op("vector", lambda e, sb=sb, ci=ci: e.tensor_mul(
                            pT[sb][:, :], pT[sb][:, :], cmask[:, ci * 512:(ci + 1) * 512]),
                            waits=[("ss", texp)], tag=f"mk{qt}_{h}_{c}")
                        tmsk = p.delay("vector", inc="sv")
                        dep = ("sv", tmsk)
                    exp_hist.append(dep)
                    wlp = [dep]
                    if c == 0:
                        if tok_prjE and h < 2:
                            wlp.append(("sv", tok_prjE))
                        hkey = (qt, h - 2) if h >= 2 else ((qt - 1, NH - 2 + h) if qt >= 1 else None)
                        if hkey is not None and hkey in evA:
                            wlp.append(("sv", evA[hkey]))
                    tpv = p.op("tensor", lambda e, avb=avb, h=h, c=c, sb=sb, nch=nch:
                               e.matmul(pb[avb][0:65, :],
                                        vaug[:, c * 1040 + h * 65: c * 1040 + h * 65 + 65],
                                        pT[sb][:, :], start=(c == 0), stop=(c == nch - 1)),
                               waits=wlp, inc="st", tag=f"pv{qt}_{h}_{c}")
                    pv_hist.append(tpv)
                    gj += 1
                p.delay("vector", waits=[("st", pv_hist[-1])])
                p.op("vector", lambda e, avb=avb, h=h: e.tensor_copy(
                    ARaw[0:65, h * 512:(h + 1) * 512], pb[avb][0:65, :]), tag=f"avE{qt}_{h}")
                evA[(qt, h)] = p.delay("vector", inc="sv")
                wlz = [("sv", evA[(qt, h)])]
                if tok_prjE and h == 0:
                    wlz.append(("sv", tok_prjE))
                if last_ln_ss:
                    wlz.append(("ss", last_ln_ss))
                tz = p.op("tensor", lambda e, h=h: e.matmul(pb[4][0:64, :], onesf[64:65, 0:64],
                                                            ARaw[64:65, h * 512:(h + 1) * 512],
                                                            start=True, stop=True),
                          waits=wlz, inc="st", tag=f"zb{qt}_{h}")
                p.delay("scalar", waits=[("st", tz)])
                last_ln_ss = p.op("scalar", lambda e: e.activation(zscr[0:64, :], pb[4][0:64, :], AF.Ln),
                                  inc="ss", tag=f"zl{qt}_{h}")
                p.op("scalar", lambda e: e.activation(zr[0:64, :], zscr[0:64, :], AF.Exp, scale=-1.0),
                     tag=f"zx{qt}_{h}")
                trc = p.delay("scalar", inc="ss")
                p.op("vector", lambda e, h=h: e.tensor_mul(
                    anT[0:64, h * 512:(h + 1) * 512], ARaw[0:64, h * 512:(h + 1) * 512], zr[0:64, :]),
                    waits=[("ss", trc)], tag=f"nm{qt}_{h}")
                nrm_tok = p.delay("vector", inc="sv")
            wp_toks = {}
            prj = {}
            for h in range(2):
                wl = []
                if h in prev_prj:
                    wl.append(("st", prev_prj[h]))
                wp_toks[h] = p.dma("gpsimd", wp_b[h % 2][0:64, :], wproj_in[h * HD:(h + 1) * HD, :], "sg",
                                   waits=wl, tag=f"wp{qt}_{h}")
            for h in range(NH):
                if h >= 2:
                    wp_toks[h] = p.dma("gpsimd", wp_b[h % 2][0:64, :], wproj_in[h * HD:(h + 1) * HD, :], "sg",
                                       waits=[("st", prj[h - 2])], tag=f"wp{qt}_{h}")
                wl = [("sg", wp_toks[h]), ("sv", nrm_tok), ("ss", last_exp_ss)]
                for qs in range(4):
                    for hh in range(2):
                        prj[h] = p.op("tensor", lambda e, h=h, qs=qs, hh=hh:
                                      e.matmul(pb[2 * qs + hh][:, :],
                                               anT[0:64, h * 512 + qs * 128: h * 512 + qs * 128 + 128],
                                               wp_b[h % 2][0:64, hh * 512:(hh + 1) * 512],
                                               start=(h == 0), stop=(h == NH - 1)),
                                      waits=(wl if (qs == 0 and hh == 0) else []),
                                      inc=("st" if (qs == 3 and hh == 1) else None),
                                      tag=f"pj{qt}_{h}_{qs}_{hh}")
            prev_prj = {0: prj[NH - 2], 1: prj[NH - 1]}
            p.delay("vector", waits=[("st", prj[NH - 1])])
            for qs in range(4):
                p.op("vector", lambda e, qs=qs: e.tensor_add(att_ev[:, 0:512], pb[2 * qs][:, :], bbc[:, 0:512]),
                     tag=f"pE{qt}_{qs}a")
                tev = p.op("vector", lambda e, qs=qs: e.tensor_add(att_ev[:, 512:1024], pb[2 * qs + 1][:, :],
                                                                   bbc[:, 512:1024]), inc="sv", tag=f"pE{qt}_{qs}b")
                row0 = qt * 512 + qs * 128
                tst = p.dma("scalar", scr_attn[row0:row0 + 128, :], att_ev[:, :], "sa",
                            waits=[("sv", tev)], tag=f"pS{qt}_{qs}")
                tok_prjE = p.op("vector", lambda e: e.nop(), waits=[("sa", tst)], inc="sv", tag="pW")

        p.barrier()

        for i in range(NT):
            t1 = p.op("tensor", lambda e, i=i: e.transpose(pb[0][0:1, 0:128], B1cols[:, i:i + 1], idf[:, :]),
                      inc="st", tag=f"B1t{i}")
            p.delay("vector", waits=[("st", t1)])
            p.op("vector", lambda e: e.tensor_copy(rows[0:1, 0:128], pb[0][0:1, 0:128]), tag=f"B1e{i}")
            t2 = p.delay("vector", inc="sv")
            t3 = p.op("tensor", lambda e: e.matmul(pb[1][:, 0:128], onesf[0:1, :], rows[0:1, 0:128],
                                                   start=True, stop=True),
                      waits=[("sv", t2)], inc="st", tag=f"B1b{i}")
            p.delay("vector", waits=[("st", t3)])
            p.op("vector", lambda e, i=i: e.tensor_copy(B1[:, i * 128:(i + 1) * 128], pb[1][:, 0:128]),
                 inc="sv", tag=f"B1c{i}")
        tok_B1 = p.cnt["sv"]

        for tb in range(TB):
            t0 = tb * 512
            w1_toks = {}
            hev = {}
            hmm = {}
            silu_toks = {}
            macc_tok = None
            for hc in range(32):
                wl = [("sa", tok_w1t)]
                if hc >= 2:
                    wl.append(("st", hmm[hc - 2]))
                w1_toks[hc] = p.dma("sync", w1t_b[hc % 2][:, :].rearrange("p (d c) -> p d c", d=8),
                                    scr_w1t[hc, :, :, :].rearrange("d p c -> p d c"), "sd", waits=wl,
                                    tag=f"w1L{tb}_{hc}")
                hb = pb[hc % 2]
                hlb = pb[2 + hc % 2]
                lo, hi = t0 - 1, t0 + 512
                wl2 = [("sd", w1_toks[hc]), ("sv", tok_B1)]
                if hc >= 2:
                    wl2.append(("sv", hev[hc - 2]))
                have_halo = (lo >= 0) or (hi < S)
                for d in range(8):
                    hmm[hc] = p.op("tensor", lambda e, hb=hb, hc=hc, d=d, t0=t0:
                                   e.matmul(hb[:, 0:512], w1t_b[hc % 2][:, d * 128:(d + 1) * 128],
                                            I1T[:, d * S + t0: d * S + t0 + 512], start=(d == 0), stop=(d == 7)),
                                   waits=(wl2 if d == 0 else []),
                                   inc=("st" if (d == 7 and not have_halo) else None), tag=f"hm{tb}_{hc}_{d}")
                    hinc = "st" if d == 7 else None
                    if lo >= 0 and hi < S:
                        hmm[hc] = p.op("tensor", lambda e, hlb=hlb, hc=hc, d=d, lo=lo:
                                       e.matmul(hlb[:, 0:2], w1t_b[hc % 2][:, d * 128:(d + 1) * 128],
                                                I1T[:, d * S + lo: d * S + lo + 514: 513],
                                                start=(d == 0), stop=(d == 7)), inc=hinc, tag=f"hh{tb}_{hc}_{d}")
                    elif lo >= 0:
                        hmm[hc] = p.op("tensor", lambda e, hlb=hlb, hc=hc, d=d, lo=lo:
                                       e.matmul(hlb[:, 0:1], w1t_b[hc % 2][:, d * 128:(d + 1) * 128],
                                                I1T[:, d * S + lo: d * S + lo + 1],
                                                start=(d == 0), stop=(d == 7)), inc=hinc, tag=f"hh{tb}_{hc}_{d}")
                    elif hi < S:
                        hmm[hc] = p.op("tensor", lambda e, hlb=hlb, hc=hc, d=d, hi=hi:
                                       e.matmul(hlb[:, 0:1], w1t_b[hc % 2][:, d * 128:(d + 1) * 128],
                                                I1T[:, d * S + hi: d * S + hi + 1],
                                                start=(d == 0), stop=(d == 7)), inc=hinc, tag=f"hh{tb}_{hc}_{d}")
                hbuf = h_sb[hc % 2]
                p.delay("vector", waits=[("st", hmm[hc])])
                p.op("vector", lambda e, hbuf=hbuf: e.memset(hbuf[:, 0:1], 0.0), tag="hz0")
                p.op("vector", lambda e, hbuf=hbuf: e.memset(hbuf[:, 513:514], 0.0), tag="hz1")
                if lo >= 0 and hi < S:
                    p.op("vector", lambda e, hlb=hlb, hbuf=hbuf, lo=lo: e.tensor_mul(
                        hbuf[:, 0:514:513], hlb[:, 0:2], B1[:, lo:lo + 514:513]), tag="hevh")
                elif lo >= 0:
                    p.op("vector", lambda e, hlb=hlb, hbuf=hbuf, lo=lo: e.tensor_mul(
                        hbuf[:, 0:1], hlb[:, 0:1], B1[:, lo:lo + 1]), tag="hevh")
                elif hi < S:
                    p.op("vector", lambda e, hlb=hlb, hbuf=hbuf, hi=hi: e.tensor_mul(
                        hbuf[:, 513:514], hlb[:, 0:1], B1[:, hi:hi + 1]), tag="hevh")
                p.op("vector", lambda e, hb=hb, hbuf=hbuf, t0=t0: e.tensor_mul(
                    hbuf[:, 1:513], hb[:, 0:512], B1[:, t0:t0 + 512]), tag="hev")
                hev[hc] = p.delay("vector", inc="sv")
                p.op("vector", lambda e, hbuf=hbuf, hc=hc: e.tensor_scalar_mul(
                    cv1[:, :], hbuf[:, 0:512], cw[:, 3 * hc:3 * hc + 1]), tag="cvA")
                p.op("vector", lambda e, hbuf=hbuf, hc=hc: e.scalar_tensor_tensor(
                    cv1[:, :], hbuf[:, 1:513], cw[:, 3 * hc + 1:3 * hc + 2], cv1[:, :], OP.mult, OP.add),
                    tag="cvB")
                tcv = p.op("vector", lambda e, hbuf=hbuf, hc=hc: e.scalar_tensor_tensor(
                    cv1[:, :], hbuf[:, 2:514], cw[:, 3 * hc + 2:3 * hc + 3], cv1[:, :], OP.mult, OP.add),
                    inc="sv", tag="cvC")
                silu_toks[hc] = p.op("scalar", lambda e, hc=hc: e.activation(
                    G[:, hc * 512:(hc + 1) * 512], cv1[:, :], AF.Silu, bias=cb[:, hc:hc + 1]),
                    waits=[("sv", tcv)], inc="ss", tag=f"si{tb}_{hc}")
                if hc == 0:
                    p.op("vector", lambda e: e.tensor_copy(macc[:, :], G[:, 0:512]),
                         waits=[("ss", silu_toks[0])], tag="mc0a")
                    macc_tok = p.op("vector", lambda e: e.tensor_copy(macc2[:, :], G[:, 0:512]),
                                    inc="sv", tag="mc0b")
                else:
                    p.op("vector", lambda e, hc=hc: e.tensor_tensor(
                        macc[:, :], macc[:, :], G[:, hc * 512:(hc + 1) * 512], OP.max),
                        waits=[("ss", silu_toks[hc])], tag=f"mcA{hc}")
                    macc_tok = p.op("vector", lambda e, hc=hc: e.tensor_tensor(
                        macc2[:, :], macc2[:, :], G[:, hc * 512:(hc + 1) * 512], OP.min),
                        inc="sv", tag=f"mcB{hc}")
            p.op("vector", lambda e: e.tensor_scalar_mul(macc2[:, :], macc2[:, :], -1.0),
                 waits=[("sv", macc_tok)], tag="mcNeg")
            macc_tok = p.op("vector", lambda e: e.tensor_max(macc[:, :], macc[:, :], macc2[:, :]),
                            inc="sv", tag="mcFin")
            trn = []
            for j in range(4):
                trn.append(p.op("tensor", lambda e, j=j: e.transpose(
                    pb[4 + j][:, 0:128], macc[:, j * 128:(j + 1) * 128], idf[:, :]),
                    waits=[("sv", macc_tok)], inc="st", tag=f"s2T{j}"))
            for j in range(4):
                p.delay("vector", waits=[("st", trn[j])])
                p.op("vector", lambda e, j=j: e.reduce_max(M2cols[:, j:j + 1], pb[4 + j][:, 0:128], axis=AX.X),
                     tag=f"s2R{j}")
            p.delay("vector")
            p.op("vector", lambda e: e.tensor_scalar_max(M2cols[:, 0:4], M2cols[:, 0:4], 1e-5), tag="s2c")
            ts2c = p.delay("vector", inc="sv")
            p.op("scalar", lambda e: e.activation(tiny[:, 0:4], M2cols[:, 0:4], AF.Ln),
                 waits=[("sv", ts2c)], tag="s2l")
            p.delay("scalar")
            p.op("scalar", lambda e: e.activation(tiny[:, 0:4], tiny[:, 0:4], AF.Exp,
                                                  scale=-1.0, bias=scal[:, 5:6]), tag="s2e")
            ts2e = p.delay("scalar", inc="ss")
            p.op("vector", lambda e: e.tensor_scalar_mul(tiny[:, 4:8], tiny[:, 0:4], 1.0 / 127.0),
                 waits=[("ss", ts2e)], tag="s2q1")
            p.delay("vector")
            p.op("vector", lambda e: e.tensor_mul(tiny[:, 4:8], M2cols[:, 0:4], tiny[:, 4:8]), tag="s2q2")
            p.delay("vector")
            p.op("vector", lambda e: e.tensor_scalar(tiny[:, 4:8], tiny[:, 4:8], -1.0, 2.0, OP.mult, OP.add),
                 tag="s2q3")
            p.delay("vector")
            p.op("vector", lambda e: e.tensor_mul(M2cols[:, 4:8], tiny[:, 0:4], tiny[:, 4:8]), tag="s2q4")
            p.delay("vector")
            p.op("vector", lambda e: e.tensor_scalar_mul(M2cols[:, 0:4], M2cols[:, 0:4], scal[:, 3:4]),
                 tag="s2B2")
            tB2 = p.delay("vector", inc="sv")
            for j in range(4):
                t1 = p.op("tensor", lambda e, j=j: e.transpose(pb[6][0:1, 0:128], M2cols[:, 4 + j:5 + j],
                                                               idf[:, :]),
                          waits=[("sv", tB2)], inc="st", tag=f"s2bT{j}")
                p.delay("vector", waits=[("st", t1)])
                p.op("vector", lambda e: e.tensor_copy(rows[0:1, 0:128], pb[6][0:1, 0:128]), tag=f"s2bE{j}")
                t2 = p.delay("vector", inc="sv")
                t3 = p.op("tensor", lambda e: e.matmul(pb[7][:, 0:128], onesf[0:1, :], rows[0:1, 0:128],
                                                       start=True, stop=True),
                          waits=[("sv", t2)], inc="st", tag=f"s2bB{j}")
                p.delay("vector", waits=[("st", t3)])
                p.op("vector", lambda e, j=j: e.tensor_copy(s2bc[:, j * 128:(j + 1) * 128], pb[7][:, 0:128]),
                     inc="sv", tag=f"s2bC{j}")
            tok_s2 = p.cnt["sv"]
            i2_toks = {}
            for hc in range(32):
                p.op("vector", lambda e, hc=hc: e.tensor_mul(cv1[:, :], G[:, hc * 512:(hc + 1) * 512],
                                                             s2bc[:, :]),
                     waits=([("sv", tok_s2)] if hc == 0 else []), tag=f"i2A{hc}")
                p.op("vector", lambda e, hc=hc: e.tensor_scalar(
                    I2[:, hc * 512:(hc + 1) * 512], cv1[:, :], MAGIC, MAGIC, OP.add, OP.subtract),
                    tag=f"i2B{hc}")
                i2_toks[hc] = p.delay("vector", inc="sv")
            w2mm = {}
            for hc in range(32):
                wl = [("sa", tok_w2t), ("sv", i2_toks[hc])]
                if hc >= 2:
                    wl.append(("st", w2mm[hc - 2]))
                tk = p.dma("sync", w2t_b[hc % 2][:, :], scr_w2t[hc, :, :], "sd", waits=wl, tag=f"w2L{tb}_{hc}")
                for tsi in range(4):
                    for hh in range(2):
                        w2mm[hc] = p.op("tensor", lambda e, hc=hc, tsi=tsi, hh=hh:
                                        e.matmul(pb[2 * tsi + hh][:, :],
                                                 I2[:, hc * 512 + tsi * 128: hc * 512 + tsi * 128 + 128],
                                                 w2t_b[hc % 2][:, hh * 512:(hh + 1) * 512],
                                                 start=(hc == 0), stop=(hc == 31)),
                                        waits=([("sd", tk), ("sv", i2_toks[hc])]
                                               if (tsi == 0 and hh == 0) else []),
                                        inc=("st" if (tsi == 3 and hh == 1) else None),
                                        tag=f"ym{tb}_{hc}_{tsi}_{hh}")
            st_toks = {}
            ev_toks = {}
            for tsi in range(4):
                row0 = t0 + tsi * 128
                wlx = []
                if tsi >= 2:
                    wlx.append(("sv", ev_toks[tsi - 2]))
                tx = p.dma("sync", xres[tsi % 2][:, :],
                           x_in[sq * S + row0: sq * S + row0 + 128, :], "sd", waits=wlx, tag=f"xr{tb}_{tsi}")
                ta = p.dma("sync", ares[tsi % 2][:, :], scr_attn[row0:row0 + 128, :], "sd",
                           waits=wlx + [("sa", p.cnt["sa"])], tag=f"ar{tb}_{tsi}")
                wl = [("st", w2mm[31]), ("sd", tx), ("sd", ta)]
                if tsi >= 2:
                    wl.append(("sa", st_toks[tsi - 2]))
                p.delay("vector", waits=wl)
                p.op("vector", lambda e, tsi=tsi: e.scalar_tensor_tensor(
                    yev[tsi % 2][:, 0:512], pb[2 * tsi][:, :], M2cols[:, tsi:tsi + 1],
                    xres[tsi % 2][:, 0:512], OP.mult, OP.add), tag=f"ye{tb}_{tsi}a")
                p.op("vector", lambda e, tsi=tsi: e.scalar_tensor_tensor(
                    yev[tsi % 2][:, 512:1024], pb[2 * tsi + 1][:, :], M2cols[:, tsi:tsi + 1],
                    xres[tsi % 2][:, 512:1024], OP.mult, OP.add), tag=f"ye{tb}_{tsi}b")
                ev_toks[tsi] = p.op("vector", lambda e, tsi=tsi: e.tensor_add(
                    yev[tsi % 2][:, :], yev[tsi % 2][:, :], ares[tsi % 2][:, :]), inc="sv", tag=f"ye{tb}_{tsi}c")
                st_toks[tsi] = p.dma("scalar", out_hbm[sq * S + row0: sq * S + row0 + 128, :],
                                     yev[tsi % 2][:, :], "sa", waits=[("sv", ev_toks[tsi])], tag=f"o{tb}_{tsi}")
            prev_out_tok = st_toks[3]
            p.barrier()

    p.op("sync", lambda e: e.nop(), waits=[("sa", p.cnt["sa"]), ("sd", p.cnt["sd"]), ("sg", p.cnt["sg"])],
         tag="final")


_NC_CACHE = {}


def kernel(L0_attn_x, L0_attn_w_qkv, L0_attn_w_proj, L0_attn_b_proj,
           L0_ffn_w1, L0_ffn_conv_w, L0_ffn_conv_b, L0_ffn_w2):
    x = np.ascontiguousarray(np.asarray(L0_attn_x, dtype=np.float32))
    B, S, Dm = x.shape
    NCORES = 8
    NS = B // NCORES
    key = (S, NS)
    if key not in _NC_CACHE:
        _NC_CACHE[key] = build_nc(S=S, NS=NS)
    nc = _NC_CACHE[key]
    consts = host_constants(S)
    base = dict(
        w_qkv=np.ascontiguousarray(L0_attn_w_qkv, dtype=np.float32),
        w_proj=np.ascontiguousarray(L0_attn_w_proj, dtype=np.float32),
        b_proj=np.ascontiguousarray(L0_attn_b_proj, dtype=np.float32),
        w1=np.ascontiguousarray(L0_ffn_w1, dtype=np.float32),
        conv_w=np.ascontiguousarray(L0_ffn_conv_w, dtype=np.float32),
        conv_b=np.ascontiguousarray(L0_ffn_conv_b, dtype=np.float32),
        w2=np.ascontiguousarray(L0_ffn_w2, dtype=np.float32),
        ident=consts["ident"], cosT=np.ascontiguousarray(consts["cosT"]),
        sinS=np.ascontiguousarray(consts["sinS"]),
        prot=np.ascontiguousarray(consts["prot"]), cmask=np.ascontiguousarray(consts["cmask"]),
    )
    in_maps = []
    for c in range(NCORES):
        m = dict(base)
        m["x"] = np.ascontiguousarray(x[c * NS:(c + 1) * NS].reshape(NS * S, Dm))
        in_maps.append(m)
    res = run_bass_kernel_spmd(nc, in_maps, core_ids=list(range(NCORES)))
    out = np.concatenate([np.asarray(r["out"], np.float32).reshape(NS, S, Dm)
                          for r in res.results], axis=0)
    return out


# version 16
# speedup vs baseline: 1.4620x; 1.4620x over previous
import math
import numpy as np
import ml_dtypes
from contextlib import ExitStack

import concourse.bass as bass
import concourse.mybir as mybir
from concourse.bass_utils import run_bass_kernel_spmd
from concourse.alu_op_type import AluOpType as OP

F32 = mybir.dt.float32
BF16 = mybir.dt.bfloat16
AF = mybir.ActivationFunctionType
AX = mybir.AxisListType
MAGIC = 12582912.0
LN127 = math.log(127.0)

D = 1024
HID = 4096
NH = 16
HD = 64


class Op:
    __slots__ = ("eng", "fn", "waits", "inc_sem", "inc_amt", "tag")

    def __init__(self, eng, fn, waits, inc_sem, inc_amt, tag):
        self.eng, self.fn, self.waits = eng, fn, waits
        self.inc_sem, self.inc_amt, self.tag = inc_sem, inc_amt, tag


class Prog:
    ENGS = ("sync", "gpsimd", "tensor", "vector", "scalar")

    def __init__(self, nc):
        self.nc = nc
        self.ops = []
        self.sems = {}
        self.cnt = {}

    def sem(self, name, ctx):
        self.sems[name] = ctx.enter_context(self.nc.semaphore(name))
        self.cnt[name] = 0
        return name

    AUTODELAY = ("vector", "scalar", "gpsimd")

    def op(self, eng, fn, waits=(), inc=None, amt=1, raw_inc=False, tag=""):
        wl = []
        for (s, vv) in waits:
            assert vv <= self.cnt[s], f"future token {s}={vv}>{self.cnt[s]} tag={tag}"
            if vv > 0:
                wl.append((s, vv))
        if inc is not None and not raw_inc and eng in self.AUTODELAY and eng in getattr(self, "dly", {}):
            self.ops.append(Op(eng, fn, wl, None, 1, tag))
            buf = self.dly[eng]
            half = buf.shape[1] // 2
            if eng == "scalar":
                dfn = lambda e, half=half: e.copy(buf[:, half:2 * half], buf[:, 0:half])
            else:
                dfn = lambda e, half=half: e.tensor_copy(buf[:, half:2 * half], buf[:, 0:half])
            self.cnt[inc] += amt
            tok = self.cnt[inc]
            self.ops.append(Op(eng, dfn, [], inc, amt, tag + "+d"))
            return tok
        tok = None
        if inc is not None:
            self.cnt[inc] += amt
            tok = self.cnt[inc]
        self.ops.append(Op(eng, fn, wl, inc, amt, tag))
        return tok

    def dma(self, queue, out, in_, sem, waits=(), transpose=False, slow_ok=False, tag=""):
        def fn(eng, out=out, in_=in_, transpose=transpose, slow_ok=slow_ok):
            if transpose:
                return eng.dma_start(out=out, in_=in_, transpose=True)
            if slow_ok:
                return eng.dma_start(out=out, in_=in_, allow_slow_non_contiguous=True)
            return eng.dma_start(out=out, in_=in_)
        return self.op(queue, fn, waits=waits, inc=sem, amt=16, raw_inc=True, tag=tag)

    def tok(self, eng, sem, waits=(), tag="tok"):
        return self.op(eng, lambda e: e.nop(), waits=waits, inc=sem, raw_inc=True, tag=tag)

    def delay(self, eng, inc=None, waits=(), tag="dly"):
        buf = self.dly[eng]
        half = buf.shape[1] // 2
        if eng == "scalar":
            fn = lambda e: e.copy(buf[:, half:2 * half], buf[:, 0:half])
        else:
            fn = lambda e: e.tensor_copy(buf[:, half:2 * half], buf[:, 0:half])
        return self.op(eng, fn, inc=inc, waits=waits, raw_inc=True, tag=tag)

    def barrier(self):
        toks = {}
        for e in self.ENGS:
            if e == "sync":
                toks[e] = self.op(e, lambda en: en.nop(), inc="b_" + e, raw_inc=True, tag="bar")
            else:
                toks[e] = self.op(e, lambda en: en.drain(), inc="b_" + e, raw_inc=True, tag="bar")
        for e in self.ENGS:
            waits = [("b_" + o, toks[o]) for o in self.ENGS if o != e]
            self.op(e, lambda en: en.nop(), waits=waits, tag="barw")

    def replay(self, block):
        sems = self.sems

        def make(name):
            def run(eng):
                for o in self.ops:
                    if o.eng != name:
                        continue
                    for (s, vv) in o.waits:
                        eng.wait_ge(sems[s], vv)
                    inst = o.fn(eng)
                    if o.inc_sem is not None:
                        inst.then_inc(sems[o.inc_sem], o.inc_amt)
            return run

        block.sync(make("sync"))
        block.gpsimd(make("gpsimd"))
        block.tensor(make("tensor"))
        block.vector(make("vector"))
        block.scalar(make("scalar"))


def host_constants(S):
    c = {}
    c["ident"] = np.eye(128, dtype=np.float32)
    inv = 1.0 / (10000.0 ** (np.arange(0, HD, 2) / HD))
    ang = np.arange(S)[None, :] * inv[:, None]
    cos = np.cos(ang)
    sin = np.sin(ang)
    c["cosT"] = np.concatenate([cos, cos, cos, cos], 0).astype(ml_dtypes.bfloat16)
    c["sinS"] = np.concatenate([-sin, sin, -sin, sin], 0).astype(ml_dtypes.bfloat16)
    pr = np.zeros((128, 128), dtype=np.float32)
    for o in range(128):
        pr[(o % 64 + 32) % 64 + 64 * (o // 64), o] = 1.0
    c["prot"] = pr.astype(ml_dtypes.bfloat16)
    k = np.arange(128)[:, None]
    j = np.arange(512)[None, :]
    c["cmask"] = np.concatenate([(i * 128 + k <= j) for i in range(4)], 1).astype(ml_dtypes.bfloat16)
    return c


def build_nc(S=2048, NS=2):
    NT = S // 128
    TB = S // 512
    nc = bass.Bass("TRN2", target_bir_lowering=False, detect_race_conditions=False)

    x_in = nc.dram_tensor("x", [NS * S, D], F32, kind="ExternalInput").ap()
    wqkv_in = nc.dram_tensor("w_qkv", [D, 3 * D], F32, kind="ExternalInput").ap()
    wproj_in = nc.dram_tensor("w_proj", [D, D], F32, kind="ExternalInput").ap()
    bproj_in = nc.dram_tensor("b_proj", [D], F32, kind="ExternalInput").ap()
    w1_in = nc.dram_tensor("w1", [HID, D], F32, kind="ExternalInput").ap()
    cw_in = nc.dram_tensor("conv_w", [HID, 1, 3], F32, kind="ExternalInput").ap()
    cb_in = nc.dram_tensor("conv_b", [HID], F32, kind="ExternalInput").ap()
    w2_in = nc.dram_tensor("w2", [D, HID], F32, kind="ExternalInput").ap()
    ident_in = nc.dram_tensor("ident", [128, 128], F32, kind="ExternalInput").ap()
    cosT_in = nc.dram_tensor("cosT", [128, S], BF16, kind="ExternalInput").ap()
    sinS_in = nc.dram_tensor("sinS", [128, S], BF16, kind="ExternalInput").ap()
    prot_in = nc.dram_tensor("prot", [128, 128], BF16, kind="ExternalInput").ap()
    cmask_in = nc.dram_tensor("cmask", [128, 2048], BF16, kind="ExternalInput").ap()
    out_hbm = nc.dram_tensor("out", [NS * S, D], F32, kind="ExternalOutput").ap()

    scr_xnbf = nc.dram_tensor("scr_xnbf", [S, D], BF16, kind="Internal").ap()
    scr_i1 = nc.dram_tensor("scr_i1", [S, D], BF16, kind="Internal").ap()
    scr_w1i = nc.dram_tensor("scr_w1i", [HID, D], BF16, kind="Internal").ap()
    scr_w2i = nc.dram_tensor("scr_w2i", [D, HID], BF16, kind="Internal").ap()
    scr_w1t = nc.dram_tensor("scr_w1t", [HID // 128, D // 128, 128, 128], BF16, kind="Internal").ap()
    scr_w2t = nc.dram_tensor("scr_w2t", [HID // 128, 128, D], BF16, kind="Internal").ap()
    scr_attn = nc.dram_tensor("scr_attn", [S, D], F32, kind="Internal").ap()

    ptr = [16512]

    def At(name, shape, dtype, off=None):
        sz = int(np.prod(shape[1:])) * (4 if dtype == F32 else 2)
        if off is None:
            off = ptr[0]
            ptr[0] = off + ((sz + 31) // 32) * 32
        return nc.alloc_sbuf_tensor_at(name, shape, dtype, offset=off)

    v = {}
    v["idf"] = At("idf", [128, 128], F32)
    v["prot"] = At("prot_s", [128, 128], BF16)
    v["onesf"] = At("onesf", [128, 128], F32)
    v["cmask"] = At("cmask_s", [128, 2048], BF16)
    v["bbc"] = At("bbc", [128, D], F32)
    v["cw"] = At("cw", [128, 96], F32)
    v["cb"] = At("cb", [128, 32], F32)
    v["scal"] = At("scal", [128, 8], F32)
    v["B1cols"] = At("B1cols", [128, NT], F32)
    v["rows"] = At("rows", [128, 128], F32)
    v["tiny"] = At("tiny", [128, 8], F32)
    v["M2cols"] = At("M2cols", [128, 8], F32)
    v["lncols"] = At("lncols", [128, 224], F32)
    v["dly_v"] = At("dly_v", [128, 512], F32)
    v["dly_s"] = At("dly_s", [128, 512], F32)
    v["dly_g"] = At("dly_g", [128, 448], F32)
    v["asums"] = At("asums", [128, 32], F32)

    v["I1T"] = At("I1T", [128, 8 * S], BF16)
    reg_x = ptr[0]
    v["xnT"] = At("xnT", [128, 8 * S], BF16, off=reg_x)
    v["ARaw"] = At("ARaw", [128, NH * 512], F32, off=reg_x)
    v["I2"] = At("I2", [128, 32 * 512], BF16, off=reg_x)
    ptr[0] = reg_x + max(16 * S, 32768, 32768)
    reg_qg = ptr[0]
    v["QKT"] = At("QKT", [128, 16 * S], BF16, off=reg_qg)
    v["G"] = At("G", [128, 32 * 512], F32, off=reg_qg)
    ptr[0] = reg_qg + max(32 * S, 65536)
    reg_m = ptr[0]

    def Am(name, shape, dtype, off):
        return At(name, shape, dtype, off=reg_m + off)

    v["bpr"] = Am("bpr", [1, D], F32, 0)
    v["xld"] = [Am(f"xld{i}", [128, D], F32, i * 4096) for i in range(3)]
    v["xn_buf"] = Am("xn_buf", [128, D], F32, 12288)
    v["tmpA1"] = Am("tmpA1", [128, D], F32, 16384)
    v["i1t_b"] = [Am(f"i1t{i}", [128, D], BF16, 20480 + i * 2048) for i in range(2)]
    v["xnb_b"] = [Am(f"xnb{i}", [128, D], BF16, 24576 + i * 2048) for i in range(2)]
    v["wtile"] = [At(f"wtile{i}", [128, HID], F32, off=reg_qg + i * 16384) for i in range(2)]
    v["wq_b"] = [At(f"wq{i}", [128, HID], BF16, off=reg_qg + 32768 + i * 8192) for i in range(2)]
    v["wtt_b"] = [At(f"wtt{i}", [128, HID], BF16, off=reg_qg + 49152 + i * 8192) for i in range(2)]
    v["wqkv_b"] = [Am(f"wqkvb{i}", [128, 128], BF16, 28672 + i * 256) for i in range(2)]
    v["rtmpA"] = Am("rtmpA", [128, 512], BF16, 29184)
    v["rtmpB"] = Am("rtmpB", [128, 512], BF16, 30208)
    v["cosT"] = Am("cosT_s", [128, S], BF16, 31232)
    v["sinS"] = Am("sinS_s", [128, S], BF16, 31232 + 2 * S)
    v["vaug"] = Am("vaug", [128, NH * 65 * (S // 128)], BF16, 0)
    v["wv_sb"] = Am("wv_sb", [128, 8 * D], BF16, 33280)
    v["anT"] = Am("anT", [128, NH * 512], BF16, 33280)
    v["pT"] = [Am(f"pT{i}", [128, 512], BF16, 49664 + i * 1024) for i in range(2)]
    v["zr"] = Am("zr", [128, 512], F32, 51712)
    v["zscr"] = Am("zscr", [128, 512], F32, 53760)
    v["wp_b"] = [Am(f"wp{i}", [128, D], BF16, 55808 + i * 2048) for i in range(2)]
    v["att_ev"] = Am("attev", [128, D], F32, 59904)
    v["B1"] = Am("B1", [128, S], F32, 0)
    v["w1t_b"] = [Am(f"w1t{i}", [128, 1024], BF16, 8192 + i * 2048) for i in range(2)]
    v["w2t_b"] = [Am(f"w2t{i}", [128, D], BF16, 12288 + i * 2048) for i in range(2)]
    v["h_sb"] = [Am(f"hsb{i}", [128, 516], F32, 16384 + i * 2080) for i in range(2)]
    v["cv1"] = Am("cv1", [128, 512], F32, 20608)
    v["macc"] = Am("macc", [128, 512], F32, 22656)
    v["macc2"] = Am("macc2", [128, 512], F32, 51328)
    v["s2bc"] = Am("s2bc", [128, 512], F32, 24704)
    v["yev"] = [Am(f"yev{i}", [128, D], F32, 26752 + i * 4096) for i in range(2)]
    v["xres"] = [Am(f"xres{i}", [128, D], F32, 34944 + i * 4096) for i in range(2)]
    v["ares"] = [Am(f"ares{i}", [128, D], F32, 43136 + i * 4096) for i in range(2)]
    assert S >= 1024 or True
    assert reg_m + 64000 <= 229376, f"SBUF overflow {reg_m + 64000}"

    v["pb"] = [nc.alloc_psum_tensor(f"pb{i}", [128, 512], F32) for i in range(8)]
    for k in ("x_in wqkv_in wproj_in bproj_in w1_in cw_in cb_in w2_in ident_in cosT_in "
              "sinS_in prot_in cmask_in out_hbm scr_xnbf scr_i1 scr_w1i scr_w2i scr_w1t "
              "scr_w2t scr_attn").split():
        v[k] = locals()[k]
    v["S"], v["NS"], v["NT"], v["TB"] = S, NS, NT, TB

    with ExitStack() as ctx:
        p = Prog(nc)
        p.dly = {"vector": v["dly_v"], "scalar": v["dly_s"], "gpsimd": v["dly_g"]}
        for s in ("sd", "sa", "sg", "st", "sv", "ss", "sc"):
            p.sem(s, ctx)
        for e in Prog.ENGS:
            p.sem("b_" + e, ctx)
        block = ctx.enter_context(nc.Block())
        emit(p, v)
        p.replay(block)
    return nc


def emit(p, v):
    S, NS, NT, TB = v["S"], v["NS"], v["NT"], v["TB"]
    pb = v["pb"]
    idf, prot, onesf = v["idf"], v["prot"], v["onesf"]
    cosT, sinS, cmask = v["cosT"], v["sinS"], v["cmask"]
    bbc, cw, cb, scal = v["bbc"], v["cw"], v["cb"], v["scal"]
    B1cols, rows, bpr = v["B1cols"], v["rows"], v["bpr"]
    tiny, M2cols = v["tiny"], v["M2cols"]
    lncols = v["lncols"]
    I1T, xnT, ARaw, I2, QKT, G = v["I1T"], v["xnT"], v["ARaw"], v["I2"], v["QKT"], v["G"]
    xld, xn_buf, tmpA1 = v["xld"], v["xn_buf"], v["tmpA1"]
    i1t_b, xnb_b = v["i1t_b"], v["xnb_b"]
    wtile, asums, wq_b, wtt_b = v["wtile"], v["asums"], v["wq_b"], v["wtt_b"]
    wqkv_b, wv_sb, rtmpA, rtmpB = v["wqkv_b"], v["wv_sb"], v["rtmpA"], v["rtmpB"]
    vaug, anT, pT, zr, zscr = v["vaug"], v["anT"], v["pT"], v["zr"], v["zscr"]
    wp_b, att_ev = v["wp_b"], v["att_ev"]
    w1t_b, w2t_b, h_sb, cv1 = v["w1t_b"], v["w2t_b"], v["h_sb"], v["cv1"]
    macc, s2bc, B1 = v["macc"], v["s2bc"], v["B1"]
    macc2 = v["macc2"]
    yev, xres, ares = v["yev"], v["xres"], v["ares"]
    x_in, wqkv_in, wproj_in, bproj_in = v["x_in"], v["wqkv_in"], v["wproj_in"], v["bproj_in"]
    w1_in, cw_in, cb_in, w2_in = v["w1_in"], v["cw_in"], v["cb_in"], v["w2_in"]
    ident_in, cosT_in, sinS_in = v["ident_in"], v["cosT_in"], v["sinS_in"]
    prot_in, cmask_in, out_hbm = v["prot_in"], v["cmask_in"], v["out_hbm"]
    scr_xnbf, scr_i1 = v["scr_xnbf"], v["scr_i1"]
    scr_w1i, scr_w2i, scr_w1t, scr_w2t = v["scr_w1i"], v["scr_w2i"], v["scr_w1t"], v["scr_w2t"]
    scr_attn = v["scr_attn"]

    p.op("vector", lambda e: e.memset(p.dly["vector"][:, :], 0.0), raw_inc=True, tag="dlyz_v0")
    p.op("gpsimd", lambda e: e.memset(p.dly["gpsimd"][:, :], 0.0), raw_inc=True, tag="dlyz_g0")
    p.op("scalar", lambda e: e.memzero(p.dly["scalar"][:, :]), raw_inc=True, tag="dlyz_s0")
    for (dst, src) in [(idf[:, :], ident_in),
                       (prot[:, :], prot_in), (cmask[:, :], cmask_in),
                       (bpr[:, :], bproj_in.rearrange("(o d) -> o d", o=1))]:
        tok_const = p.dma("sync", dst, src, "sd", tag="const")
    p.dma("sync", cw[:, :].rearrange("p (ch o j) -> p ch o j", ch=32, o=1),
          cw_in.rearrange("(ch p) o j -> p ch o j", p=128), "sd", slow_ok=True, tag="cw")
    tok_const = p.dma("sync", cb[:, :], cb_in.rearrange("(ch p) -> p ch", p=128), "sd",
                      slow_ok=True, tag="cb")
    tok_ones = p.op("gpsimd", lambda e: e.memset(onesf[:, :], 1.0), inc="sc", tag="ones")

    p.op("gpsimd", lambda e: e.memset(scal[:, 7:8], -MAGIC), inc="sc", tag="negmag")
    p.op("gpsimd", lambda e: e.memset(scal[:, 5:6], LN127), inc="sc", tag="ln127")

    t = p.op("tensor", lambda e: e.matmul(pb[0][:, :], onesf[0:1, :], bpr[0:1, 0:512], start=True, stop=True),
             waits=[("sd", tok_const), ("sc", tok_ones)], tag="bbc0")
    t = p.op("tensor", lambda e: e.matmul(pb[1][:, :], onesf[0:1, :], bpr[0:1, 512:1024], start=True, stop=True),
             inc="st", tag="bbc1")
    p.delay("vector", waits=[("st", t)])
    p.op("vector", lambda e: e.tensor_copy(bbc[:, 0:512], pb[0][:, :]), tag="bbcE0")
    tok_bbc = p.op("vector", lambda e: e.tensor_copy(bbc[:, 512:1024], pb[1][:, :]), inc="sv", tag="bbcE1")

    def wquant(win, rowsN, colsN, scr_int, col_s, col_m):
        ntile = rowsN // 128
        red_toks = {}
        for i in range(ntile):
            wl = []
            if i >= 2:
                wl.append(("sv", red_toks[i - 2]))
            tk = p.dma("sync", wtile[i % 2][:, 0:colsN], win[i * 128:(i + 1) * 128, 0:colsN], "sd",
                       waits=wl, tag=f"wqL{i}")
            red_toks[i] = p.op("vector", lambda e, i=i: e.reduce_sum(
                asums[:, i:i + 1], wtile[i % 2][:, 0:colsN], axis=AX.X, apply_absolute_value=True),
                waits=[("sd", tk)], inc="sv", tag=f"wqS{i}")
        t1 = p.op("vector", lambda e, n=ntile: e.reduce_sum(tiny[:, 0:1], asums[:, 0:n], axis=AX.X),
                  inc="sv", tag="wqRS")
        t1 = p.delay("vector", inc="sv")
        t2 = p.op("tensor", lambda e: e.transpose(pb[2][0:1, 0:128], tiny[:, 0:1], idf[:, :]),
                  waits=[("sv", t1)], inc="st", tag="wqT")
        p.delay("vector", waits=[("st", t2)])
        p.op("vector", lambda e: e.tensor_copy(rows[0:1, 0:128], pb[2][0:1, 0:128]), tag="wqTE")
        p.delay("vector")
        p.op("vector", lambda e: e.reduce_sum(rows[0:1, 1:2], rows[0:1, 0:128], axis=AX.X), tag="wqS2")
        p.delay("vector")
        p.op("vector", lambda e, n=1.0 / (rowsN * colsN): e.tensor_scalar_mul(rows[0:1, 2:3], rows[0:1, 1:2], n),
             tag="wqMW")
        tmw = p.delay("vector", inc="sv")
        p.op("scalar", lambda e: e.activation(rows[0:1, 7:8], rows[0:1, 2:3], AF.Ln),
             waits=[("sv", tmw)], tag="wqLn")
        p.delay("scalar")
        p.op("scalar", lambda e: e.activation(rows[0:1, 3:4], rows[0:1, 7:8], AF.Exp, scale=-1.0),
             tag="wqEx")
        tre = p.delay("scalar", inc="ss")
        p.op("vector", lambda e: e.tensor_mul(rows[0:1, 4:5], rows[0:1, 2:3], rows[0:1, 3:4]),
             waits=[("ss", tre)], tag="wqN1")
        p.delay("vector")
        p.op("vector", lambda e: e.tensor_scalar(rows[0:1, 4:5], rows[0:1, 4:5], -1.0, 2.0, OP.mult, OP.add),
             tag="wqN2")
        p.delay("vector")
        p.op("vector", lambda e: e.tensor_mul(rows[0:1, 5:6], rows[0:1, 3:4], rows[0:1, 4:5]), tag="wqN3")
        p.op("vector", lambda e: e.tensor_scalar_mul(rows[0:1, 6:7], rows[0:1, 2:3], 1.0 / 127.0),
             tag="wqM127")
        t5 = p.delay("vector", inc="sv")
        t6 = p.op("tensor", lambda e: e.matmul(pb[3][:, 0:1], onesf[0:1, :], rows[0:1, 5:6], start=True, stop=True),
                  waits=[("sv", t5)], tag="wqB1")
        t6 = p.op("tensor", lambda e: e.matmul(pb[3][:, 1:2], onesf[0:1, :], rows[0:1, 6:7], start=True, stop=True),
                  inc="st", tag="wqB2")
        p.delay("vector", waits=[("st", t6)])
        p.op("vector", lambda e, c=col_s: e.tensor_copy(scal[:, c:c + 1], pb[3][:, 0:1]), tag="wqBE1")
        p.op("vector", lambda e, c=col_m: e.tensor_copy(scal[:, c:c + 1], pb[3][:, 1:2]), tag="wqBE2")
        t7 = p.delay("vector", inc="sv")
        q_toks = {}
        st_toks = {}
        tok_store = None
        for i in range(ntile):
            wl = [("sv", t7)]
            if i >= 2:
                wl.append(("sv", q_toks[i - 2]))
            tk = p.dma("sync", wtile[i % 2][:, 0:colsN], win[i * 128:(i + 1) * 128, 0:colsN], "sd",
                       waits=wl, tag=f"wq2L{i}")
            p.op("vector", lambda e, i=i, c=col_s, n=colsN: e.tensor_scalar(
                wtile[i % 2][:, 0:n], wtile[i % 2][:, 0:n], scal[:, c:c + 1], MAGIC, OP.mult, OP.add),
                waits=[("sd", tk)], tag=f"wq2A{i}")
            p.op("vector", lambda e, i=i, n=colsN: e.tensor_scalar(
                wtile[i % 2][:, 0:n], wtile[i % 2][:, 0:n], MAGIC, 1.0, OP.subtract, OP.min), tag=f"wq2B{i}")
            wl2 = []
            if i >= 2:
                wl2.append(("sa", st_toks[i - 2]))
            q_toks[i] = p.op("vector", lambda e, i=i, n=colsN: e.tensor_scalar(
                wq_b[i % 2][:, 0:n], wtile[i % 2][:, 0:n], -1.0, None, OP.max),
                waits=wl2, inc="sv", tag=f"wq2C{i}")
            tok_store = p.dma("scalar", scr_int[i * 128:(i + 1) * 128, 0:colsN], wq_b[i % 2][:, 0:colsN], "sa",
                              waits=[("sv", q_toks[i])], tag=f"wq2S{i}")
            st_toks[i] = tok_store
        return tok_store

    tok_w1i = wquant(w1_in, HID, D, scr_w1i, 0, 1)
    tok_w2i = wquant(w2_in, D, HID, scr_w2i, 2, 3)

    prev = {}
    tok_w1t = None
    for d in range(8):
        wl = [("sa", tok_w1i)]
        if d >= 2:
            wl.append(("sa", prev[d - 2]))
        tk = p.dma("sync", wtt_b[d % 2][:, 0:4096], scr_w1i[:, d * 128:(d + 1) * 128], "sd",
                   waits=wl, transpose=True, tag=f"w1tT{d}")
        tok_w1t = p.dma("scalar", scr_w1t[:, d, :, :].rearrange("hc p c -> p hc c"),
                        wtt_b[d % 2][:, 0:4096].rearrange("p (hc c) -> p hc c", hc=32), "sa",
                        waits=[("sd", tk)], tag=f"w1tS{d}")
        prev[d] = tok_w1t
    prev = {}
    tok_w2t = None
    for hc in range(32):
        wl = [("sa", tok_w2i), ("sa", tok_w1t)]
        if hc >= 2:
            wl.append(("sa", prev[hc - 2]))
        tk = p.dma("sync", wtt_b[hc % 2][:, 0:1024], scr_w2i[:, hc * 128:(hc + 1) * 128], "sd",
                   waits=wl, transpose=True, tag=f"w2tT{hc}")
        tok_w2t = p.dma("scalar", scr_w2t[hc, :, :], wtt_b[hc % 2][:, 0:1024], "sa",
                        waits=[("sd", tk)], tag=f"w2tS{hc}")
        prev[hc] = tok_w2t

    p.barrier()

    prev_out_tok = None
    for sq in range(NS):
        xs_base = sq * S
        wq_pre = {}
        for k in (0, 1):
            wq_pre[k] = p.dma("gpsimd", wqkv_b[k][:, :],
                              wqkv_in[k * 128:(k + 1) * 128, 0:128], "sg",
                              waits=([] if prev_out_tok is None else [("sa", prev_out_tok)]),
                              tag=f"qwpre{k}")
        ld_toks = {}
        sq_toks = {}
        red_toks = {}
        for i in range(min(3, NT)):
            ld_toks[i] = p.dma("sync", xld[i % 3][:, :], x_in[xs_base + i * 128: xs_base + (i + 1) * 128, :],
                               "sd", tag=f"lnL{i}")
        for i in range(NT):
            b = xld[i % 3]
            wl = [("sd", ld_toks[i])]
            wsq = list(wl)
            if i >= 1:
                wsq.append(("sv", sq_toks[i - 1]))
            tsq0 = p.op("scalar", lambda e, b=b: e.activation(xn_buf[:, :], b[:, :], AF.Square),
                        waits=wsq, inc="ss", tag="lnSQ")
            p.op("vector", lambda e, i=i: e.reduce_sum(lncols[:, 16 + i:17 + i], xn_buf[:, :], axis=AX.X),
                 waits=[("ss", tsq0)], tag="lnSQr")
            sq_toks[i] = p.op("vector", lambda e, b=b, i=i: e.reduce_sum(lncols[:, i:i + 1], b[:, :], axis=AX.X),
                              waits=wl, inc="sv", tag="lnS")
            p.op("vector", lambda e, b=b, i=i: e.reduce_max(lncols[:, 32 + i:33 + i], b[:, :], axis=AX.X),
                 tag="lnMX")
            red_toks[i] = p.op("vector", lambda e, b=b, i=i: e.tensor_reduce(lncols[:, 48 + i:49 + i], b[:, :],
                                                                             axis=AX.X, op=OP.min),
                               inc="sv", tag="lnMN")
            if i + 3 < NT:
                ld_toks[i + 3] = p.dma("sync", xld[(i + 3) % 3][:, :],
                                       x_in[xs_base + (i + 3) * 128: xs_base + (i + 4) * 128, :], "sd",
                                       waits=[("sv", red_toks[i])], tag=f"lnL{i + 3}")
        C = lncols
        n = NT
        D_ = p.delay
        p.op("vector", lambda e: e.tensor_scalar_mul(C[:, 64:64 + n], C[:, 0:n], 1.0 / D), tag="p2m")
        D_("vector")
        p.op("vector", lambda e: e.tensor_mul(C[:, 80:80 + n], C[:, 64:64 + n], C[:, 64:64 + n]), tag="p2mm")
        p.op("vector", lambda e: e.tensor_scalar_mul(C[:, 96:96 + n], C[:, 16:16 + n], 1.0 / D), tag="p2sq")
        D_("vector")
        p.op("vector", lambda e: e.tensor_sub(C[:, 96:96 + n], C[:, 96:96 + n], C[:, 80:80 + n]), tag="p2var")
        D_("vector")
        p.op("vector", lambda e: e.tensor_scalar_add(C[:, 96:96 + n], C[:, 96:96 + n], 1e-5), tag="p2vpe")
        tvpe = D_("vector", inc="sv")
        p.op("scalar", lambda e: e.activation(C[:, 192:192 + n], C[:, 96:96 + n], AF.Ln),
             waits=[("sv", tvpe)], tag="p2ln")
        D_("scalar")
        p.op("scalar", lambda e: e.activation(C[:, 112:112 + n], C[:, 192:192 + n], AF.Exp, scale=-0.5),
             tag="p2r0")
        tr0 = D_("scalar", inc="ss")
        p.op("vector", lambda e: e.tensor_mul(C[:, 80:80 + n], C[:, 112:112 + n], C[:, 112:112 + n]),
             waits=[("ss", tr0)], tag="p2n1")
        D_("vector")
        p.op("vector", lambda e: e.tensor_mul(C[:, 80:80 + n], C[:, 80:80 + n], C[:, 96:96 + n]), tag="p2n2")
        D_("vector")
        p.op("vector", lambda e: e.tensor_scalar(C[:, 80:80 + n], C[:, 80:80 + n], -0.5, 1.5, OP.mult, OP.add),
             tag="p2n3")
        D_("vector")
        p.op("vector", lambda e: e.tensor_mul(C[:, 128:128 + n], C[:, 112:112 + n], C[:, 80:80 + n]), tag="p2r")
        p.op("vector", lambda e: e.tensor_sub(C[:, 176:176 + n], C[:, 32:32 + n], C[:, 64:64 + n]), tag="p2dm")
        p.op("vector", lambda e: e.tensor_sub(C[:, 80:80 + n], C[:, 64:64 + n], C[:, 48:48 + n]), tag="p2dn")
        D_("vector")
        p.op("vector", lambda e: e.tensor_max(C[:, 176:176 + n], C[:, 176:176 + n], C[:, 80:80 + n]), tag="p2M1")
        D_("vector")
        p.op("vector", lambda e: e.tensor_mul(C[:, 176:176 + n], C[:, 176:176 + n], C[:, 128:128 + n]), tag="p2M1r")
        D_("vector")
        p.op("vector", lambda e: e.tensor_scalar_max(C[:, 144:144 + n], C[:, 176:176 + n], 1e-5), tag="p2M1c")
        tM1 = D_("vector", inc="sv")
        p.op("scalar", lambda e: e.activation(C[:, 192:192 + n], C[:, 144:144 + n], AF.Ln),
             waits=[("sv", tM1)], tag="p2l2")
        D_("scalar")
        p.op("scalar", lambda e: e.activation(C[:, 208:208 + n], C[:, 192:192 + n], AF.Exp,
                                              scale=-1.0, bias=scal[:, 5:6]),
             tag="p2s10")
        ts10 = D_("scalar", inc="ss")
        p.op("vector", lambda e: e.tensor_scalar_mul(C[:, 80:80 + n], C[:, 208:208 + n], 1.0 / 127.0),
             waits=[("ss", ts10)], tag="p2q1")
        D_("vector")
        p.op("vector", lambda e: e.tensor_mul(C[:, 80:80 + n], C[:, 144:144 + n], C[:, 80:80 + n]), tag="p2q2")
        D_("vector")
        p.op("vector", lambda e: e.tensor_scalar(C[:, 80:80 + n], C[:, 80:80 + n], -1.0, 2.0, OP.mult, OP.add),
             tag="p2q3")
        D_("vector")
        p.op("vector", lambda e: e.tensor_mul(C[:, 160:160 + n], C[:, 208:208 + n], C[:, 80:80 + n]), tag="p2s1")
        D_("vector")
        p.op("vector", lambda e: e.tensor_scalar_mul(B1cols[:, 0:n], C[:, 144:144 + n], scal[:, 1:2]),
             tag="p2B1")
        tp2 = D_("vector", inc="sv")
        ld2 = {}
        i1_toks = {}
        xnb_toks = {}
        tok_lnst = None
        for i in range(min(3, NT)):
            ld2[i] = p.dma("sync", xld[i % 3][:, :], x_in[xs_base + i * 128: xs_base + (i + 1) * 128, :],
                           "sd", waits=[("sv", red_toks[NT - 1])], tag=f"ln3L{i}")
        for i in range(NT):
            b = xld[i % 3]
            wxn = [("sd", ld2[i]), ("sv", tp2)]
            if i >= 1:
                wxn.append(("sv", i1_toks[i - 1]))
                wxn.append(("ss", xnb_toks[i - 1]))
            p.op("vector", lambda e, b=b, i=i: e.tensor_scalar(xn_buf[:, :], b[:, :], C[:, 64 + i:65 + i],
                                                               C[:, 128 + i:129 + i], OP.subtract, OP.mult),
                 waits=wxn, tag="ln3XN")
            i1_toks[i] = p.op("vector", lambda e, i=i: e.tensor_scalar(tmpA1[:, :], xn_buf[:, :],
                                                                      C[:, 160 + i:161 + i], MAGIC,
                                                                      OP.mult, OP.add), inc="sv", tag="ln3I1A")
            wst = [("sv", i1_toks[i])]
            if i >= 2:
                wst.append(("sa", p.cnt["sa"]))
            p.op("scalar", lambda e, i=i: e.activation(i1t_b[i % 2][:, :], tmpA1[:, :], AF.Identity,
                                                       bias=scal[:, 7:8]), waits=wst, tag="ln3I1B")
            xnb_toks[i] = p.op("scalar", lambda e, i=i: e.copy(xnb_b[i % 2][:, :], xn_buf[:, :]),
                               inc="ss", tag="ln3XNB")
            p.dma("scalar", scr_i1[i * 128:(i + 1) * 128, :], i1t_b[i % 2][:, :], "sa",
                  waits=[("ss", xnb_toks[i])], tag=f"ln3SI{i}")
            tok_lnst = p.dma("scalar", scr_xnbf[i * 128:(i + 1) * 128, :], xnb_b[i % 2][:, :], "sa",
                             tag=f"ln3SX{i}")
            if i + 3 < NT:
                ld2[i + 3] = p.dma("sync", xld[(i + 3) % 3][:, :],
                                   x_in[xs_base + (i + 3) * 128: xs_base + (i + 4) * 128, :], "sd",
                                   waits=[("sv", i1_toks[i])], tag=f"ln3L{i + 3}")
        tok_tr = None
        txn = {}
        for d in range(8):
            txn[d] = p.dma("sync", xnT[:, d * S:(d + 1) * S], scr_xnbf[:, d * 128:(d + 1) * 128], "sd",
                           waits=[("sa", tok_lnst)], transpose=True, tag=f"trX{d}")
            tok_tr = p.dma("sync", I1T[:, d * S:(d + 1) * S], scr_i1[:, d * 128:(d + 1) * 128], "sd",
                           transpose=True, tag=f"trI{d}")

        p.barrier()

        ev_tok = {}
        k_idx = 0
        mm_g = {}
        wq_toks = {}
        for oc in range(16):
            pset = (oc % 2) * 4
            for d in range(8):
                if k_idx < 2:
                    wq_toks[k_idx] = wq_pre[k_idx]
                else:
                    wq_toks[k_idx] = p.dma("gpsimd", wqkv_b[k_idx % 2][:, :],
                                           wqkv_in[d * 128:(d + 1) * 128, oc * 128:(oc + 1) * 128],
                                           "sg", waits=[("st", mm_g[k_idx - 2])], tag=f"qw{oc}_{d}")
                wl2 = [("sg", wq_toks[k_idx]), ("sd", txn[d])]
                if d == 0 and oc >= 2:
                    wl2.append(("ss", ev_tok[oc - 2]))
                for tb in range(TB):
                    tk = p.op("tensor", lambda e, pset=pset, tb=tb, k=k_idx, d=d:
                              e.matmul(pb[pset + tb][:, :], wqkv_b[k % 2][:, :],
                                       xnT[:, d * S + tb * 512: d * S + tb * 512 + 512],
                                       start=(d == 0), stop=(d == 7)),
                              waits=(wl2 if tb == 0 else []),
                              inc=("st" if tb == TB - 1 else None), tag=f"qmm{oc}_{d}_{tb}")
                mm_g[k_idx] = tk
                k_idx += 1
            wl_ev = [("st", mm_g[k_idx - 1])]
            if oc == 0:
                wl_ev.append(("sa", tok_w2t))
            p.delay("scalar", waits=wl_ev)
            for tb in range(TB):
                ev_tok[oc] = p.op("scalar", lambda e, oc=oc, tb=tb, pset=pset:
                                  e.copy(QKT[:, oc * S + tb * 512: oc * S + tb * 512 + 512],
                                         pb[pset + tb][:, :]),
                                  inc=("ss" if tb == TB - 1 else None), raw_inc=True,
                                  tag=f"qev{oc}_{tb}")
        tok_qkev = ev_tok[15]
        p.dma("sync", cosT[:, :], cosT_in, "sd", tag="cosL")
        tok_rtab = p.dma("sync", sinS[:, :], sinS_in, "sd", tag="sinL")
        rj = 0
        cmb = {}
        for oc in range(16):
            for tb in range(TB):
                sl = slice(oc * S + tb * 512, oc * S + tb * 512 + 512)
                tsl = slice(tb * 512, tb * 512 + 512)
                wl = [("ss", tok_qkev)]
                if rj >= 2:
                    wl.append(("sv", cmb[rj - 2]))
                trt = p.op("tensor", lambda e, sl=sl, rj=rj:
                           e.matmul(pb[rj % 2][:, :], prot[:, :], QKT[:, sl], start=True, stop=True),
                           waits=wl, inc="st", tag=f"rp{rj}")
                p.op("vector", lambda e, sl=sl, tsl=tsl: e.tensor_mul(rtmpA[:, :], QKT[:, sl], cosT[:, tsl]),
                     waits=[("st", trt), ("sd", tok_rtab)], tag="rpA")
                p.op("vector", lambda e, tsl=tsl, rj=rj: e.tensor_mul(rtmpB[:, :], pb[rj % 2][:, :], sinS[:, tsl]),
                     tag="rpB")
                cmb[rj] = p.op("vector", lambda e, sl=sl: e.tensor_add(QKT[:, sl], rtmpA[:, :], rtmpB[:, :]),
                               inc="sv", raw_inc=True, tag="rpC")
                rj += 1
        tok_rope = p.delay("vector", inc="sv")
        tok_wv = None
        for d in range(8):
            tok_wv = p.dma("gpsimd", wv_sb[:, d * D:(d + 1) * D],
                           wqkv_in[d * 128:(d + 1) * 128, 2048:3072], "sg",
                           waits=[("sv", tok_rope)], tag=f"wv{d}")
        vev = {}
        vones = {}
        for i in range(NT):
            pset = 4 + (i % 2) * 2
            wl = [("sg", tok_wv), ("sd", tok_tr)]
            if i >= 2:
                wl.append(("ss", vev[i - 2]))
            for d in range(8):
                for hh in range(2):
                    tmm = p.op("tensor", lambda e, pset=pset, i=i, d=d, hh=hh:
                               e.matmul(pb[pset + hh][:, :],
                                        xnT[:, d * S + i * 128: d * S + i * 128 + 128],
                                        wv_sb[:, d * D + hh * 512: d * D + hh * 512 + 512],
                                        start=(d == 0), stop=(d == 7)),
                               waits=(wl if (d == 0 and hh == 0) else []),
                               inc=("st" if (d == 7 and hh == 1) else None), tag=f"vmm{i}_{d}_{hh}")

            def vev_fn(e, i=i, pset=pset):
                dst = vaug[:, i * 1040:(i + 1) * 1040].rearrange("p (h c) -> p h c", h=16)[:, :, 0:64]
                e.copy(dst[:, 0:8, :], pb[pset][:, :].rearrange("p (h c) -> p h c", h=8))
                return e.copy(dst[:, 8:16, :], pb[pset + 1][:, :].rearrange("p (h c) -> p h c", h=8))
            p.delay("scalar", waits=[("st", tmm)])
            vev[i] = p.op("scalar", vev_fn, inc="ss", raw_inc=True, tag=f"vev{i}")
            vones[i] = p.op("gpsimd", lambda e, i=i: e.memset(
                vaug[:, i * 1040:(i + 1) * 1040].rearrange("p (h c) -> p h c", h=16)[:, :, 64:65], 1.0),
                waits=[("ss", vev[i])], inc="sc", tag=f"von{i}")
        tok_vaug = vones[NT - 1]

        p.barrier()

        SBK = (0, 1, 5, 6)
        RING = 4
        NPT = 2
        LAG = 2
        exp_hist = []
        pv_hist = {}
        gj = 0
        last_exp_ss = 0
        evA = {}
        prev_prj = {}
        zl_state = [0]
        nrm_state = [0]
        tok_prjE = 0
        for qt in range(S // 512):
            nch = 4 * (qt + 1)
            qt_gj0 = gj
            pend = []

            def flush(n):
                while sum(1 for k, _ in pend if k == "pv") > n or (n == 0 and pend):
                    pend.pop(0)[1]()

            for h in range(NH):
                base = (h % 2) * 64
                qoff = (h // 2) * S + qt * 512
                koff = (8 + h // 2) * S
                avb = 2 + (h % 2)
                for c in range(nch):
                    sb = SBK[gj % RING]
                    sl = gj % NPT
                    wl = [("sv", tok_rope), ("sc", tok_vaug)]
                    if tok_prjE and h == 0 and c < RING:
                        wl.append(("sv", tok_prjE))
                    if gj >= RING:
                        wl.append(exp_hist[gj - RING])
                    tmm = p.op("tensor", lambda e, base=base, koff=koff, qoff=qoff, c=c, sb=sb:
                               e.matmul(pb[sb][:, :],
                                        QKT[:, koff + c * 128: koff + c * 128 + 128][base:base + 64, :],
                                        QKT[:, qoff: qoff + 512][base:base + 64, :],
                                        start=True, stop=True),
                               waits=wl, inc="st", tag=f"sT{qt}_{h}_{c}")
                    flush(LAG - 1)
                    wle = [("st", tmm)]
                    if gj >= NPT:
                        wle.append(("st", pv_hist[gj - NPT]))
                    efn = lambda e, sb=sb, sl=sl: e.activation(pT[sl][:, :], pb[sb][:, :],
                                                               AF.Exp, scale=0.125)
                    if gj - NPT >= qt_gj0:
                        texp = p.op("scalar", efn, waits=wle, inc="ss", raw_inc=True,
                                    tag=f"ex{qt}_{h}_{c}")
                    else:
                        p.delay("scalar", waits=wle)
                        texp = p.op("scalar", efn, inc="ss", raw_inc=True, tag=f"ex{qt}_{h}_{c}")
                    last_exp_ss = texp
                    dep = ("ss", texp)
                    if c >= 4 * qt:
                        ci = c - 4 * qt
                        tmsk = p.op("vector", lambda e, sl=sl, ci=ci: e.tensor_mul(
                            pT[sl][:, :], pT[sl][:, :], cmask[:, ci * 512:(ci + 1) * 512]),
                            waits=[("ss", texp)], inc="sv", raw_inc=True, tag=f"mk{qt}_{h}_{c}")
                        dep = ("sv", tmsk)
                    exp_hist.append(dep)

                    def emit_pv(dep=dep, h=h, c=c, sl=sl, avb=avb, nch=nch, gj=gj):
                        wlp = [dep]
                        if c == 0:
                            if tok_prjE and h < 2:
                                wlp.append(("sv", tok_prjE))
                            hkey = (qt, h - 2) if h >= 2 else ((qt - 1, NH - 2 + h) if qt >= 1 else None)
                            if hkey is not None and hkey in evA:
                                wlp.append(("sv", evA[hkey]))
                        pv_hist[gj] = p.op("tensor", lambda e: e.matmul(
                            pb[avb][0:65, :],
                            vaug[:, c * 1040 + h * 65: c * 1040 + h * 65 + 65],
                            pT[sl][:, :], start=(c == 0), stop=(c == nch - 1)),
                            waits=wlp, inc="st", tag=f"pv{qt}_{h}_{c}")

                    pend.append(("pv", emit_pv))
                    if c == nch - 1:
                        def emit_tail(h=h, avb=avb, gjl=gj):
                            p.delay("vector", waits=[("st", pv_hist[gjl])])
                            evA[(qt, h)] = p.op("vector", lambda e: e.tensor_copy(
                                ARaw[0:65, h * 512:(h + 1) * 512], pb[avb][0:65, :]),
                                inc="sv", raw_inc=True, tag=f"avE{qt}_{h}")
                            wlz = [("sv", evA[(qt, h)])]
                            if tok_prjE and h == 0:
                                wlz.append(("sv", tok_prjE))
                            if zl_state[0]:
                                wlz.append(("ss", zl_state[0]))
                            tz = p.op("tensor", lambda e: e.matmul(pb[4][0:64, :], onesf[64:65, 0:64],
                                                                   ARaw[64:65, h * 512:(h + 1) * 512],
                                                                   start=True, stop=True),
                                      waits=wlz, inc="st", tag=f"zb{qt}_{h}")
                            p.delay("scalar", waits=[("st", tz)])
                            zl_state[0] = p.op("scalar", lambda e: e.activation(
                                zscr[0:64, :], pb[4][0:64, :], AF.Ln), inc="ss", raw_inc=True,
                                tag=f"zl{qt}_{h}")
                            p.delay("scalar")
                            trc = p.op("scalar", lambda e: e.activation(zr[0:64, :], zscr[0:64, :],
                                                                        AF.Exp, scale=-1.0),
                                       inc="ss", raw_inc=True, tag=f"zx{qt}_{h}")
                            nrm_state[0] = p.op("vector", lambda e: e.tensor_mul(
                                anT[0:64, h * 512:(h + 1) * 512], ARaw[0:64, h * 512:(h + 1) * 512],
                                zr[0:64, :]), waits=[("ss", trc)], inc="sv", raw_inc=True,
                                tag=f"nm{qt}_{h}")

                        pend.append(("tail", emit_tail))
                    gj += 1
            flush(0)
            nrm_tok = nrm_state[0]
            wp_toks = {}
            prj = {}
            for h in range(2):
                wl = []
                if h in prev_prj:
                    wl.append(("st", prev_prj[h]))
                wp_toks[h] = p.dma("gpsimd", wp_b[h % 2][0:64, :], wproj_in[h * HD:(h + 1) * HD, :], "sg",
                                   waits=wl, tag=f"wp{qt}_{h}")
            for h in range(NH):
                if h >= 2:
                    wp_toks[h] = p.dma("gpsimd", wp_b[h % 2][0:64, :], wproj_in[h * HD:(h + 1) * HD, :], "sg",
                                       waits=[("st", prj[h - 2])], tag=f"wp{qt}_{h}")
                wl = [("sg", wp_toks[h]), ("sv", nrm_tok), ("ss", last_exp_ss)]
                for qs in range(4):
                    for hh in range(2):
                        prj[h] = p.op("tensor", lambda e, h=h, qs=qs, hh=hh:
                                      e.matmul(pb[2 * qs + hh][:, :],
                                               anT[0:64, h * 512 + qs * 128: h * 512 + qs * 128 + 128],
                                               wp_b[h % 2][0:64, hh * 512:(hh + 1) * 512],
                                               start=(h == 0), stop=(h == NH - 1)),
                                      waits=(wl if (qs == 0 and hh == 0) else []),
                                      inc=("st" if (qs == 3 and hh == 1) else None),
                                      tag=f"pj{qt}_{h}_{qs}_{hh}")
            prev_prj = {0: prj[NH - 2], 1: prj[NH - 1]}
            p.delay("vector", waits=[("st", prj[NH - 1])])
            for qs in range(4):
                p.op("vector", lambda e, qs=qs: e.tensor_add(att_ev[:, 0:512], pb[2 * qs][:, :], bbc[:, 0:512]),
                     tag=f"pE{qt}_{qs}a")
                tev = p.op("vector", lambda e, qs=qs: e.tensor_add(att_ev[:, 512:1024], pb[2 * qs + 1][:, :],
                                                                   bbc[:, 512:1024]), inc="sv", tag=f"pE{qt}_{qs}b")
                row0 = qt * 512 + qs * 128
                tst = p.dma("scalar", scr_attn[row0:row0 + 128, :], att_ev[:, :], "sa",
                            waits=[("sv", tev)], tag=f"pS{qt}_{qs}")
                tok_prjE = p.op("vector", lambda e: e.nop(), waits=[("sa", tst)], inc="sv", tag="pW")

        p.barrier()

        for i in range(NT):
            t1 = p.op("tensor", lambda e, i=i: e.transpose(pb[0][0:1, 0:128], B1cols[:, i:i + 1], idf[:, :]),
                      inc="st", tag=f"B1t{i}")
            p.delay("vector", waits=[("st", t1)])
            p.op("vector", lambda e: e.tensor_copy(rows[0:1, 0:128], pb[0][0:1, 0:128]), tag=f"B1e{i}")
            t2 = p.delay("vector", inc="sv")
            t3 = p.op("tensor", lambda e: e.matmul(pb[1][:, 0:128], onesf[0:1, :], rows[0:1, 0:128],
                                                   start=True, stop=True),
                      waits=[("sv", t2)], inc="st", tag=f"B1b{i}")
            p.delay("vector", waits=[("st", t3)])
            p.op("vector", lambda e, i=i: e.tensor_copy(B1[:, i * 128:(i + 1) * 128], pb[1][:, 0:128]),
                 inc="sv", tag=f"B1c{i}")
        tok_B1 = p.cnt["sv"]

        for tb in range(TB):
            t0 = tb * 512
            w1_toks = {}
            hev = {}
            hmm = {}
            silu_toks = {}
            macc_tok = None
            for hc in range(32):
                wl = [("sa", tok_w1t)]
                if hc >= 2:
                    wl.append(("st", hmm[hc - 2]))
                w1_toks[hc] = p.dma("sync", w1t_b[hc % 2][:, :].rearrange("p (d c) -> p d c", d=8),
                                    scr_w1t[hc, :, :, :].rearrange("d p c -> p d c"), "sd", waits=wl,
                                    tag=f"w1L{tb}_{hc}")
                hb = pb[hc % 2]
                hlb = pb[2 + hc % 2]
                lo, hi = t0 - 1, t0 + 512
                wl2 = [("sd", w1_toks[hc]), ("sv", tok_B1)]
                if hc >= 2:
                    wl2.append(("sv", hev[hc - 2]))
                have_halo = (lo >= 0) or (hi < S)
                for d in range(8):
                    hmm[hc] = p.op("tensor", lambda e, hb=hb, hc=hc, d=d, t0=t0:
                                   e.matmul(hb[:, 0:512], w1t_b[hc % 2][:, d * 128:(d + 1) * 128],
                                            I1T[:, d * S + t0: d * S + t0 + 512], start=(d == 0), stop=(d == 7)),
                                   waits=(wl2 if d == 0 else []),
                                   inc=("st" if (d == 7 and not have_halo) else None), tag=f"hm{tb}_{hc}_{d}")
                    hinc = "st" if d == 7 else None
                    if lo >= 0 and hi < S:
                        hmm[hc] = p.op("tensor", lambda e, hlb=hlb, hc=hc, d=d, lo=lo:
                                       e.matmul(hlb[:, 0:2], w1t_b[hc % 2][:, d * 128:(d + 1) * 128],
                                                I1T[:, d * S + lo: d * S + lo + 514: 513],
                                                start=(d == 0), stop=(d == 7)), inc=hinc, tag=f"hh{tb}_{hc}_{d}")
                    elif lo >= 0:
                        hmm[hc] = p.op("tensor", lambda e, hlb=hlb, hc=hc, d=d, lo=lo:
                                       e.matmul(hlb[:, 0:1], w1t_b[hc % 2][:, d * 128:(d + 1) * 128],
                                                I1T[:, d * S + lo: d * S + lo + 1],
                                                start=(d == 0), stop=(d == 7)), inc=hinc, tag=f"hh{tb}_{hc}_{d}")
                    elif hi < S:
                        hmm[hc] = p.op("tensor", lambda e, hlb=hlb, hc=hc, d=d, hi=hi:
                                       e.matmul(hlb[:, 0:1], w1t_b[hc % 2][:, d * 128:(d + 1) * 128],
                                                I1T[:, d * S + hi: d * S + hi + 1],
                                                start=(d == 0), stop=(d == 7)), inc=hinc, tag=f"hh{tb}_{hc}_{d}")
                hbuf = h_sb[hc % 2]
                p.delay("vector", waits=[("st", hmm[hc])])
                p.op("vector", lambda e, hbuf=hbuf: e.memset(hbuf[:, 0:1], 0.0), tag="hz0")
                p.op("vector", lambda e, hbuf=hbuf: e.memset(hbuf[:, 513:514], 0.0), tag="hz1")
                if lo >= 0 and hi < S:
                    p.op("vector", lambda e, hlb=hlb, hbuf=hbuf, lo=lo: e.tensor_mul(
                        hbuf[:, 0:514:513], hlb[:, 0:2], B1[:, lo:lo + 514:513]), tag="hevh")
                elif lo >= 0:
                    p.op("vector", lambda e, hlb=hlb, hbuf=hbuf, lo=lo: e.tensor_mul(
                        hbuf[:, 0:1], hlb[:, 0:1], B1[:, lo:lo + 1]), tag="hevh")
                elif hi < S:
                    p.op("vector", lambda e, hlb=hlb, hbuf=hbuf, hi=hi: e.tensor_mul(
                        hbuf[:, 513:514], hlb[:, 0:1], B1[:, hi:hi + 1]), tag="hevh")
                hev[hc] = p.op("vector", lambda e, hb=hb, hbuf=hbuf, t0=t0: e.tensor_mul(
                    hbuf[:, 1:513], hb[:, 0:512], B1[:, t0:t0 + 512]),
                    inc="sv", raw_inc=True, tag="hev")
                p.op("vector", lambda e, hbuf=hbuf, hc=hc: e.tensor_scalar_mul(
                    cv1[:, :], hbuf[:, 0:512], cw[:, 3 * hc:3 * hc + 1]), tag="cvA")
                p.op("vector", lambda e, hbuf=hbuf, hc=hc: e.scalar_tensor_tensor(
                    cv1[:, :], hbuf[:, 1:513], cw[:, 3 * hc + 1:3 * hc + 2], cv1[:, :], OP.mult, OP.add),
                    tag="cvB")
                tcv = p.op("vector", lambda e, hbuf=hbuf, hc=hc: e.scalar_tensor_tensor(
                    cv1[:, :], hbuf[:, 2:514], cw[:, 3 * hc + 2:3 * hc + 3], cv1[:, :], OP.mult, OP.add),
                    inc="sv", raw_inc=True, tag="cvC")
                silu_toks[hc] = p.op("scalar", lambda e, hc=hc: e.activation(
                    G[:, hc * 512:(hc + 1) * 512], cv1[:, :], AF.Silu, bias=cb[:, hc:hc + 1]),
                    waits=[("sv", tcv)], inc="ss", raw_inc=True, tag=f"si{tb}_{hc}")
                if hc == 0:
                    p.op("vector", lambda e: e.tensor_copy(macc[:, :], G[:, 0:512]),
                         waits=[("ss", silu_toks[0])], tag="mc0a")
                    macc_tok = p.op("vector", lambda e: e.tensor_copy(macc2[:, :], G[:, 0:512]),
                                    inc="sv", raw_inc=True, tag="mc0b")
                else:
                    p.op("vector", lambda e, hc=hc: e.tensor_tensor(
                        macc[:, :], macc[:, :], G[:, hc * 512:(hc + 1) * 512], OP.max),
                        waits=[("ss", silu_toks[hc])], tag=f"mcA{hc}")
                    macc_tok = p.op("vector", lambda e, hc=hc: e.tensor_tensor(
                        macc2[:, :], macc2[:, :], G[:, hc * 512:(hc + 1) * 512], OP.min),
                        inc="sv", raw_inc=True, tag=f"mcB{hc}")
            p.op("vector", lambda e: e.tensor_scalar_mul(macc2[:, :], macc2[:, :], -1.0),
                 waits=[("sv", macc_tok)], tag="mcNeg")
            macc_tok = p.op("vector", lambda e: e.tensor_max(macc[:, :], macc[:, :], macc2[:, :]),
                            inc="sv", raw_inc=True, tag="mcFin")
            trn = []
            for j in range(4):
                trn.append(p.op("tensor", lambda e, j=j: e.transpose(
                    pb[4 + j][:, 0:128], macc[:, j * 128:(j + 1) * 128], idf[:, :]),
                    waits=[("sv", macc_tok)], inc="st", tag=f"s2T{j}"))
            for j in range(4):
                p.delay("vector", waits=[("st", trn[j])])
                p.op("vector", lambda e, j=j: e.reduce_max(M2cols[:, j:j + 1], pb[4 + j][:, 0:128], axis=AX.X),
                     tag=f"s2R{j}")
            p.delay("vector")
            p.op("vector", lambda e: e.tensor_scalar_max(M2cols[:, 0:4], M2cols[:, 0:4], 1e-5), tag="s2c")
            ts2c = p.delay("vector", inc="sv")
            p.op("scalar", lambda e: e.activation(tiny[:, 0:4], M2cols[:, 0:4], AF.Ln),
                 waits=[("sv", ts2c)], tag="s2l")
            p.delay("scalar")
            p.op("scalar", lambda e: e.activation(tiny[:, 0:4], tiny[:, 0:4], AF.Exp,
                                                  scale=-1.0, bias=scal[:, 5:6]), tag="s2e")
            ts2e = p.delay("scalar", inc="ss")
            p.op("vector", lambda e: e.tensor_scalar_mul(tiny[:, 4:8], tiny[:, 0:4], 1.0 / 127.0),
                 waits=[("ss", ts2e)], tag="s2q1")
            p.delay("vector")
            p.op("vector", lambda e: e.tensor_mul(tiny[:, 4:8], M2cols[:, 0:4], tiny[:, 4:8]), tag="s2q2")
            p.delay("vector")
            p.op("vector", lambda e: e.tensor_scalar(tiny[:, 4:8], tiny[:, 4:8], -1.0, 2.0, OP.mult, OP.add),
                 tag="s2q3")
            p.delay("vector")
            p.op("vector", lambda e: e.tensor_mul(M2cols[:, 4:8], tiny[:, 0:4], tiny[:, 4:8]), tag="s2q4")
            p.delay("vector")
            p.op("vector", lambda e: e.tensor_scalar_mul(M2cols[:, 0:4], M2cols[:, 0:4], scal[:, 3:4]),
                 tag="s2B2")
            tB2 = p.delay("vector", inc="sv")
            for j in range(4):
                t1 = p.op("tensor", lambda e, j=j: e.transpose(pb[6][0:1, 0:128], M2cols[:, 4 + j:5 + j],
                                                               idf[:, :]),
                          waits=[("sv", tB2)], inc="st", tag=f"s2bT{j}")
                p.delay("vector", waits=[("st", t1)])
                p.op("vector", lambda e: e.tensor_copy(rows[0:1, 0:128], pb[6][0:1, 0:128]), tag=f"s2bE{j}")
                t2 = p.delay("vector", inc="sv")
                t3 = p.op("tensor", lambda e: e.matmul(pb[7][:, 0:128], onesf[0:1, :], rows[0:1, 0:128],
                                                       start=True, stop=True),
                          waits=[("sv", t2)], inc="st", tag=f"s2bB{j}")
                p.delay("vector", waits=[("st", t3)])
                p.op("vector", lambda e, j=j: e.tensor_copy(s2bc[:, j * 128:(j + 1) * 128], pb[7][:, 0:128]),
                     inc="sv", tag=f"s2bC{j}")
            tok_s2 = p.cnt["sv"]
            i2_toks = {}
            for hc in range(32):
                p.op("vector", lambda e, hc=hc: e.tensor_mul(cv1[:, :], G[:, hc * 512:(hc + 1) * 512],
                                                             s2bc[:, :]),
                     waits=([("sv", tok_s2)] if hc == 0 else []), tag=f"i2A{hc}")
                i2_toks[hc] = p.op("vector", lambda e, hc=hc: e.tensor_scalar(
                    I2[:, hc * 512:(hc + 1) * 512], cv1[:, :], MAGIC, MAGIC, OP.add, OP.subtract),
                    inc="sv", raw_inc=True, tag=f"i2B{hc}")
            w2mm = {}
            for hc in range(32):
                wl = [("sa", tok_w2t), ("sv", i2_toks[hc])]
                if hc >= 2:
                    wl.append(("st", w2mm[hc - 2]))
                tk = p.dma("sync", w2t_b[hc % 2][:, :], scr_w2t[hc, :, :], "sd", waits=wl, tag=f"w2L{tb}_{hc}")
                for tsi in range(4):
                    for hh in range(2):
                        w2mm[hc] = p.op("tensor", lambda e, hc=hc, tsi=tsi, hh=hh:
                                        e.matmul(pb[2 * tsi + hh][:, :],
                                                 I2[:, hc * 512 + tsi * 128: hc * 512 + tsi * 128 + 128],
                                                 w2t_b[hc % 2][:, hh * 512:(hh + 1) * 512],
                                                 start=(hc == 0), stop=(hc == 31)),
                                        waits=([("sd", tk), ("sv", i2_toks[hc])]
                                               if (tsi == 0 and hh == 0) else []),
                                        inc=("st" if (tsi == 3 and hh == 1) else None),
                                        tag=f"ym{tb}_{hc}_{tsi}_{hh}")
            st_toks = {}
            ev_toks = {}
            for tsi in range(4):
                row0 = t0 + tsi * 128
                wlx = []
                if tsi >= 2:
                    wlx.append(("sv", ev_toks[tsi - 2]))
                tx = p.dma("sync", xres[tsi % 2][:, :],
                           x_in[sq * S + row0: sq * S + row0 + 128, :], "sd", waits=wlx, tag=f"xr{tb}_{tsi}")
                ta = p.dma("sync", ares[tsi % 2][:, :], scr_attn[row0:row0 + 128, :], "sd",
                           waits=wlx + [("sa", p.cnt["sa"])], tag=f"ar{tb}_{tsi}")
                wl = [("st", w2mm[31]), ("sd", tx), ("sd", ta)]
                if tsi >= 2:
                    wl.append(("sa", st_toks[tsi - 2]))
                p.delay("vector", waits=wl)
                p.op("vector", lambda e, tsi=tsi: e.scalar_tensor_tensor(
                    yev[tsi % 2][:, 0:512], pb[2 * tsi][:, :], M2cols[:, tsi:tsi + 1],
                    xres[tsi % 2][:, 0:512], OP.mult, OP.add), tag=f"ye{tb}_{tsi}a")
                p.op("vector", lambda e, tsi=tsi: e.scalar_tensor_tensor(
                    yev[tsi % 2][:, 512:1024], pb[2 * tsi + 1][:, :], M2cols[:, tsi:tsi + 1],
                    xres[tsi % 2][:, 512:1024], OP.mult, OP.add), tag=f"ye{tb}_{tsi}b")
                ev_toks[tsi] = p.op("vector", lambda e, tsi=tsi: e.tensor_add(
                    yev[tsi % 2][:, :], yev[tsi % 2][:, :], ares[tsi % 2][:, :]), inc="sv", tag=f"ye{tb}_{tsi}c")
                st_toks[tsi] = p.dma("scalar", out_hbm[sq * S + row0: sq * S + row0 + 128, :],
                                     yev[tsi % 2][:, :], "sa", waits=[("sv", ev_toks[tsi])], tag=f"o{tb}_{tsi}")
            prev_out_tok = st_toks[3]
            p.barrier()

    p.op("sync", lambda e: e.nop(), waits=[("sa", p.cnt["sa"]), ("sd", p.cnt["sd"]), ("sg", p.cnt["sg"])],
         tag="final")


_NC_CACHE = {}


def kernel(L0_attn_x, L0_attn_w_qkv, L0_attn_w_proj, L0_attn_b_proj,
           L0_ffn_w1, L0_ffn_conv_w, L0_ffn_conv_b, L0_ffn_w2):
    x = np.ascontiguousarray(np.asarray(L0_attn_x, dtype=np.float32))
    B, S, Dm = x.shape
    NCORES = 8
    NS = B // NCORES
    key = (S, NS)
    if key not in _NC_CACHE:
        _NC_CACHE[key] = build_nc(S=S, NS=NS)
    nc = _NC_CACHE[key]
    consts = host_constants(S)
    base = dict(
        w_qkv=np.ascontiguousarray(L0_attn_w_qkv, dtype=np.float32),
        w_proj=np.ascontiguousarray(L0_attn_w_proj, dtype=np.float32),
        b_proj=np.ascontiguousarray(L0_attn_b_proj, dtype=np.float32),
        w1=np.ascontiguousarray(L0_ffn_w1, dtype=np.float32),
        conv_w=np.ascontiguousarray(L0_ffn_conv_w, dtype=np.float32),
        conv_b=np.ascontiguousarray(L0_ffn_conv_b, dtype=np.float32),
        w2=np.ascontiguousarray(L0_ffn_w2, dtype=np.float32),
        ident=consts["ident"], cosT=np.ascontiguousarray(consts["cosT"]),
        sinS=np.ascontiguousarray(consts["sinS"]),
        prot=np.ascontiguousarray(consts["prot"]), cmask=np.ascontiguousarray(consts["cmask"]),
    )
    in_maps = []
    for c in range(NCORES):
        m = dict(base)
        m["x"] = np.ascontiguousarray(x[c * NS:(c + 1) * NS].reshape(NS * S, Dm))
        in_maps.append(m)
    res = run_bass_kernel_spmd(nc, in_maps, core_ids=list(range(NCORES)))
    out = np.concatenate([np.asarray(r["out"], np.float32).reshape(NS, S, Dm)
                          for r in res.results], axis=0)
    return out
